# version 05
# baseline (speedup 1.0000x reference)
import numpy as np
import ml_dtypes
from contextlib import ExitStack

import concourse.bass as bass
import concourse.mybir as mybir
from concourse.bass_utils import run_bass_kernel_spmd

F32 = mybir.dt.float32
BF16 = mybir.dt.bfloat16
U8 = mybir.dt.uint8
AF = mybir.ActivationFunctionType
ALU = mybir.AluOpType
AX = mybir.AxisListType
BF = ml_dtypes.bfloat16

D = 4096
DC = 32
WQ = 1024
CC3 = 24
NH = 16
DH = 64
HID = 11008
MCH = 86
TT = 512
NT = 2
RK = 16
NEG = -1e9
EPS = 1e-5
N_CORES = 8

RING_SLOTS = 6
RING_COLS = 1152

ENGS = ("te", "ve", "sc", "sy", "gp")
W_SEMS = tuple(f"w{i}" for i in range(RING_SLOTS))
O_SEMS = tuple(f"o{i}" for i in range(4))
P_SEMS = tuple(f"p{i}" for i in range(6))
DMA_SEMS = W_SEMS + O_SEMS + P_SEMS + ("x", "y")


class Prog:

    def __init__(self):
        self.ops = {e: [] for e in ENGS}
        self.cnt = dict(x=0, y=0, te=0, ve=0, sc=0, gp=0)
        for s in O_SEMS + P_SEMS:
            self.cnt[s] = 0
        for s in W_SEMS:
            self.cnt[s] = 0
        self.lastwait = {e: {} for e in ENGS}

    def op(self, eng, meth, *args, waits=(), inc=None, **kwargs):
        wl = []
        for sem, val in waits:
            if not val:
                continue
            if sem in DMA_SEMS:
                val = self.cnt[sem]
            if self.lastwait[eng].get(sem, 0) >= val:
                continue
            self.lastwait[eng][sem] = val
            wl.append((sem, val))
        iv = None
        if inc:
            self.cnt[inc] += 16 if inc in DMA_SEMS else 1
            iv = self.cnt[inc]
        self.ops[eng].append((tuple(wl), meth, args, kwargs, inc))
        return iv

    def replay(self, ename, eng, sems):
        for wl, meth, args, kwargs, inc in self.ops[ename]:
            for sem, val in wl:
                eng.wait_ge(sems[sem], val)
            r = getattr(eng, meth)(*args, **kwargs)
            if inc:
                r.then_inc(sems[inc], 16 if inc in DMA_SEMS else 1)


class WStream:

    def __init__(self, p, buf_ap):
        self.p = p
        self.buf = buf_ap
        self.q = []
        self.freeval = [0] * RING_SLOTS
        self.issued = 0
        self.acquired = 0
        self.wv = {}

    def push(self, src, cols):
        assert cols <= RING_COLS
        self.q.append((src, cols))

    def _issue(self):
        i = self.issued
        src, cols = self.q[i]
        s = i % RING_SLOTS
        self.wv[i] = self.p.op(
            "sy", "dma_start",
            self.buf[:, s * RING_COLS: s * RING_COLS + cols], src,
            waits=(("te", self.freeval[s]),), inc=W_SEMS[s],
        )
        self.issued += 1

    def acquire(self):
        i = self.acquired
        while self.issued < min(i + RING_SLOTS, len(self.q)) :
            self._issue()
        self.acquired += 1
        s = i % RING_SLOTS
        cols = self.q[i][1]
        return (self.buf[:, s * RING_COLS: s * RING_COLS + cols],
                (W_SEMS[s], self.wv.pop(i)), s)

    def release(self, s):
        self.freeval[s] = self.p.cnt["te"]


def build_nc(stop=None):
    nc = bass.Bass("TRN2", target_bir_lowering=False)

    xs = nc.dram_tensor("xs", [1280, D], F32, kind="ExternalInput").ap()
    conv_s = nc.dram_tensor("conv_s", [CC3, 128, 96 * 128], BF16, kind="ExternalInput").ap()
    qk_s = nc.dram_tensor("qk_s", [2, 8, 128, 1024], BF16, kind="ExternalInput").ap()
    wv_s = nc.dram_tensor("wv_s", [128, 8192], BF16, kind="ExternalInput").ap()
    wo_s = nc.dram_tensor("wo_s", [8, 128, 1024], BF16, kind="ExternalInput").ap()
    pj_s = nc.dram_tensor("pj_s", [DC, 128, 1024], BF16, kind="ExternalInput").ap()
    gu_s = nc.dram_tensor("gu_s", [MCH, 2, 128, 33 * 128], BF16, kind="ExternalInput").ap()
    dn_s = nc.dram_tensor("dn_s", [DC, 128, 87 * 128], BF16, kind="ExternalInput").ap()
    lcon = nc.dram_tensor("lcon", [128, 2400], BF16, kind="ExternalInput").ap()
    biasb_in = nc.dram_tensor("biasb", [128, 122], F32, kind="ExternalInput").ap()
    out_d = nc.dram_tensor("out", [NT * TT, D], F32, kind="ExternalOutput").ap()

    es = ExitStack()

    def sb(name, shape, dtype):
        return es.enter_context(nc.sbuf_tensor(name, shape, dtype)).ap()

    def ps(name, shape, dtype):
        return es.enter_context(nc.psum_tensor(name, shape, dtype)).ap()

    r = sb("r", [128, 4 * D], F32)
    ring_t = sb("ring", [128, RING_SLOTS * RING_COLS], BF16)
    lora = sb("lora", [128, 2400], BF16)
    xabuf = sb("xabuf", [128, 512], BF16)
    biasb = sb("biasbuf", [128, 122], F32)
    id_f = sb("id_f", [128, 128], F32)
    id_b = sb("id_b", [128, 128], BF16)
    maskA = sb("maskA", [128, 128], F32)
    stage_dn = sb("stage_dn", [128, 1024], BF16)
    gsil = sb("gsil", [128, 1024], BF16)
    ssum = sb("ssum", [128, 8], F32)
    ssq = sb("ssq", [128, 8], F32)
    smean = sb("smean", [128, 8], F32)
    svar = sb("svar", [128, 8], F32)
    smsq = sb("smsq", [128, 8], F32)
    sstd = sb("sstd", [128, 8], F32)
    srinv = sb("srinv", [128, 8], F32)
    snegmr = sb("snegmr", [128, 8], F32)
    snegmax = sb("snegmax", [128, NH], F32)
    srowsum = sb("srowsum", [128, NH], F32)
    srinv16 = sb("srinv16", [128, NH], F32)

    POOL_BYTES = 120832
    pool_t = es.enter_context(nc.sbuf_tensor("pool", [128, POOL_BYTES], U8))
    pool_base = nc.lookup_mloc(pool_t).addr
    carves = []

    def carve(name, off, shape, dtype, phases):
        nbytes = int(np.prod(shape[1:])) * mybir.dt.size(dtype)
        assert off + nbytes <= POOL_BYTES, f"{name} overflows pool"
        for (n2, o2, b2, ph2) in carves:
            if off < o2 + b2 and o2 < off + nbytes and (phases & ph2):
                raise AssertionError(f"pool overlap {name} vs {n2}")
        carves.append((name, off, nbytes, phases))
        h = nc.alloc_sbuf_tensor_at(name, shape, dtype, offset=pool_base + off)
        return h.ap()

    PH = lambda *ks: frozenset(ks)
    xln = carve("xln", 0, [128, DC * 514], BF16, PH("A", "B"))
    stage_tm = carve("stage_tm", 32896, [128, 2 * D], BF16, PH("A", "G"))
    halo = carve("halo", 49280, [128, 2 * D], F32, PH("A", "J"))
    dumpA = carve("dumpA", 82048, [128, D], BF16, PH("A"))
    qkv = carve("qkv", 32896, [128, CC3 * 512], BF16, PH("B", "C"))
    wv_hold = carve("wv_hold", 57472, [128, 8192], BF16, PH("C"))
    qh = carve("qh", 73856, [128, 16 * 512], BF16, PH("C", "D"))
    kh = carve("kh", 90240, [128, 16 * 512], BF16, PH("C", "D"))
    vh = carve("vh", 106624, [128, 4 * 1024], BF16, PH("C", "D"))
    sm = carve("sm", 0, [128, NH * 128], F32, PH("D"))
    p_bf = carve("p_bf", 8192, [128, NH * 128], BF16, PH("D"))
    spT = carve("spT", 12288, [128, NH * 128], BF16, PH("D"))
    o_fm = carve("o_fm", 16384, [128, 8 * 512], BF16, PH("D", "E"))
    amix = carve("amix", 24576, [128, 8 * 512], BF16, PH("E"))
    stage_pf = carve("stage_pf", 32768, [128, 1024], F32, PH("E"))
    attn_out = carve("attn_out", 36864, [128, 4 * D], F32, PH("E", "F"))
    dumpF = carve("dumpF", 0, [128, D], BF16, PH("F"))
    xln2 = carve("xln2", 0, [128, DC * 512], BF16, PH("G", "H"))
    dumpG = carve("dumpG", 49280, [128, D], BF16, PH("G"))
    hidden = carve("hidden", 32768, [128, MCH * 512], BF16, PH("H", "I"))
    ffn_out = carve("ffn_out", 0, [128, 4 * D], BF16, PH("I", "J"))
    dumpJ = carve("dumpJ", 32768, [128, D], BF16, PH("J"))

    xln3 = xln.rearrange("p (c t) -> p c t", t=514)
    xln23 = xln2.rearrange("p (c t) -> p c t", t=512)

    pA = ps("pA", [128, 512], F32)
    pB = ps("pB", [128, 512], F32)
    pC = ps("pC", [128, 512], F32)
    pD = ps("pD", [128, 512], F32)
    pO1 = ps("pO1", [128, 512], F32)
    pO2 = ps("pO2", [128, 512], F32)
    pT1 = ps("pT1", [128, 1024], BF16)
    pT2 = ps("pT2", [128, 1024], BF16)
    scoreb = [pA, pB, pC, pD]

    sems = {}
    for s in W_SEMS + O_SEMS + P_SEMS + ("x", "y", "te", "ve", "sc", "gp"):
        sems[s] = es.enter_context(nc.semaphore("sem_" + s))

    p = Prog()
    ws = WStream(p, ring_t)
    bankfree = {}
    stfree = {0: ("te", 0), 1: ("te", 0)}
    gsil_free = {0: ("ve", 0), 1: ("ve", 0)}
    v_exp_prev = 0
    v_pT_prev = 0

    def bwait(*banks):
        return tuple(bankfree[b] for b in banks if b in bankfree)

    p.op("gp", "memset", xabuf[:, :], 0.0, inc="gp")
    p.op("gp", "memset", maskA[:, :], 0.0, inc="gp")
    p.op("gp", "affine_select", out=maskA[:, :], in_=maskA[:, :],
         compare_op=ALU.is_ge, fill=NEG, base=0, pattern=[[-1, 128]],
         channel_multiplier=1, waits=(("gp", p.cnt["gp"]),), inc="gp")
    p.op("gp", "memset", id_f[:, :], 0.0, inc="gp")
    p.op("gp", "affine_select", out=id_f[:, :], in_=id_f[:, :],
         compare_op=ALU.not_equal, fill=1.0, base=0, pattern=[[-1, 128]],
         channel_multiplier=1, waits=(("gp", p.cnt["gp"]),), inc="gp")
    v_gp = p.cnt["gp"]
    p.op("ve", "tensor_copy", id_b[:, :], id_f[:, :], waits=(("gp", v_gp),), inc="ve")
    v_idb = p.cnt["ve"]
    p.op("sc", "dma_start", lora[:, :], lcon, inc="x")
    v_lcon = p.cnt["x"]
    p.op("sc", "dma_start", biasb[:, :], biasb_in, inc="x")
    v_bias = p.cnt["x"]
    p.op("sc", "preload_act_table", AF.Sqrt)

    def ln_stats(col, src, dump, waits):
        p.op("ve", "reduce_sum", ssum[:, col:col + 1], src, axis=AX.X,
             waits=waits, inc="ve")
        v_sum = p.cnt["ve"]
        p.op("sc", "activation", dump, src, AF.Square,
             accum_out=ssq[:, col:col + 1], waits=waits, inc="sc")
        v_sq = p.cnt["sc"]
        p.op("ve", "tensor_scalar", smean[:, col:col + 1], ssum[:, col:col + 1],
             1.0 / D, None, op0=ALU.mult, waits=(("ve", v_sum),), inc="ve")
        v_mean = p.cnt["ve"]
        p.op("ve", "tensor_scalar", svar[:, col:col + 1], ssq[:, col:col + 1],
             1.0 / D, EPS, op0=ALU.mult, op1=ALU.add, waits=(("sc", v_sq),))
        p.op("ve", "tensor_tensor", smsq[:, col:col + 1], smean[:, col:col + 1],
             smean[:, col:col + 1], op=ALU.mult, waits=(("ve", v_mean),), inc="ve")
        v_msq = p.cnt["ve"]
        p.op("ve", "tensor_tensor", svar[:, col:col + 1], svar[:, col:col + 1],
             smsq[:, col:col + 1], op=ALU.subtract, waits=(("ve", v_msq),), inc="ve")
        p.op("sc", "activation", sstd[:, col:col + 1], svar[:, col:col + 1],
             AF.Sqrt, waits=(("ve", p.cnt["ve"]),), inc="sc")
        p.op("ve", "reciprocal", srinv[:, col:col + 1], sstd[:, col:col + 1],
             waits=(("sc", p.cnt["sc"]),), inc="ve")
        return p.cnt["ve"]

    ptb_free = {"pT1": ("ve", 0), "pT2": ("ve", 0)}

    def tm_to_fm(src_bf, dst3, dst_col_off, halo_col=None, waits=()):
        for grp in range(4):
            pt, ptn = (pT1, "pT1") if grp % 2 == 0 else (pT2, "pT2")
            pt3 = pt.rearrange("p (c t) -> p c t", t=128)
            for j in range(8):
                dc = grp * 8 + j
                w = waits if (grp == 0 and j == 0) else ()
                p.op("te", "transpose", pt3[:, j, :], src_bf[:, dc * 128:(dc + 1) * 128],
                     id_b[:, :],
                     waits=w + (ptb_free[ptn], ("ve", v_idb)),
                     inc="te" if j == 7 else None)
            v_t = p.cnt["te"]
            dc0 = grp * 8
            if halo_col is None:
                p.op("ve", "tensor_copy",
                     dst3[:, dc0:dc0 + 8, dst_col_off:dst_col_off + 128],
                     pt3[:, 0:8, :], waits=(("te", v_t),), inc="ve")
            else:
                p.op("ve", "tensor_copy",
                     dst3[:, dc0:dc0 + 8, dst_col_off:dst_col_off + 1],
                     pt3[:, 0:8, halo_col:halo_col + 1],
                     waits=(("te", v_t),), inc="ve")
            ptb_free[ptn] = ("ve", p.cnt["ve"])
        return p.cnt["ve"]

    class _Stop(Exception):
        pass

    stopped = []

    def dbg_dump(*aps):
        for i, apx in enumerate(aps):
            d = nc.dram_tensor(f"dbg{i}", list(apx.shape), apx.dtype,
                               kind="ExternalOutput").ap()
            p.op("sc", "dma_start", d, apx,
                 waits=(("ve", p.cnt["ve"]), ("sc", p.cnt["sc"]),
                        ("te", p.cnt["te"]), ("w", p.cnt["w"]), ("gp", p.cnt["gp"])),
                 inc="x")
        p.op("sy", "nop", waits=(("x", p.cnt["x"]),))
        stopped.append(True)
        raise _Stop()

    def maybe_stop(tag, *aps):
        if stop == tag:
            dbg_dump(*aps)

    def _emit_tiles():
        nonlocal v_exp_prev, v_pT_prev
        for t in range(NT):
            row0 = t * TT

            for cc in range(CC3):
                for b in range(12):
                    ws.push(conv_s[cc, :, b * 1024:(b + 1) * 1024], 1024)
            for pr in range(2):
                for hc in range(8):
                    ws.push(qk_s[pr, hc, :, :], 1024)
            for cc in range(8):
                ws.push(wo_s[cc, :, :], 1024)
            for dc in range(DC):
                ws.push(pj_s[dc, :, :], 1024)
            for mc in range(MCH):
                for gu in range(2):
                    ws.push(gu_s[mc, gu, :, 0:1152], 1152)
                    for b in range(3):
                        ws.push(gu_s[mc, gu, :, 1152 + b * 1024: 1152 + (b + 1) * 1024], 1024)
            for dc in range(DC):
                for b in range(10):
                    ws.push(dn_s[dc, :, b * 1024:(b + 1) * 1024], 1024)
                ws.push(dn_s[dc, :, 10240:11136], 896)

            if t == 0:
                xv = []
                for g in range(6):
                    if g == 0:
                        dst = halo[:, 0:D]
                    elif g == 5:
                        dst = halo[:, D:2 * D]
                    else:
                        dst = r[:, (g - 1) * D: g * D]
                    xv.append(("x", p.op("sc", "dma_start", dst,
                                   xs[row0 + g * 128: row0 + (g + 1) * 128, :],
                                   waits=(("x", p.cnt["x"]), ("gp", v_gp)), inc="x")))
            else:
                xv = nxt_xv
            v_xln = 0
            for g in [0, 5, 1, 2, 3, 4]:
                if g == 0:
                    src = halo[:, 0:D]
                elif g == 5:
                    src = halo[:, D:2 * D]
                else:
                    src = r[:, (g - 1) * D: g * D]
                wv = (xv[g],)
                v_rinv = ln_stats(g, src, dumpA[:, :], wv)
                st = stage_tm[:, (g % 2) * D: (g % 2 + 1) * D]
                p.op("ve", "tensor_scalar", st, src, smean[:, g:g + 1],
                     srinv[:, g:g + 1], op0=ALU.subtract, op1=ALU.mult,
                     waits=(stfree[g % 2], ("ve", v_rinv)), inc="ve")
                v_app = p.cnt["ve"]
                if g == 0:
                    v_xln = tm_to_fm(st, xln3, 0, halo_col=127, waits=(("ve", v_app),))
                elif g == 5:
                    v_xln = tm_to_fm(st, xln3, 513, halo_col=0, waits=(("ve", v_app),))
                else:
                    v_xln = tm_to_fm(st, xln3, 1 + (g - 1) * 128, waits=(("ve", v_app),))
                stfree[g % 2] = ("te", p.cnt["te"])
            v_halo_done = v_xln
            maybe_stop("A", xln[:, :], r[:, :])

            p.op("sy", "dma_start", wv_hold[:, 0:4096], wv_s[:, 0:4096],
                 waits=(("ve", v_halo_done),), inc="y")
            p.op("sy", "dma_start", wv_hold[:, 4096:8192], wv_s[:, 4096:8192], inc="y")
            v_wv = p.cnt["y"]

            for cc in range(CC3):
                pb, pbn = (pA, "pA") if cc % 2 == 0 else (pB, "pB")
                first_w = bwait(pbn) + (("ve", v_xln),)
                for b in range(12):
                    slot, wv, si = ws.acquire()
                    for ti in range(8):
                        tt_i = b * 8 + ti
                        k, dc = tt_i // 32, tt_i % 32
                        p.op("te", "matmul", pb[:, :], slot[:, ti * 128:(ti + 1) * 128],
                             xln3[:, dc, k:k + TT],
                             start=(tt_i == 0), stop=(tt_i == 95),
                             waits=(wv,) + (first_w if tt_i == 0 else ()),
                             inc="te" if ti == 7 else None)
                    ws.release(si)
                v_mm = p.cnt["te"]
                p.op("sc", "activation", qkv[:, cc * 512:(cc + 1) * 512], pb[:, :],
                     AF.Identity, bias=biasb[:, cc:cc + 1],
                     waits=(("te", v_mm), ("x", v_bias)), inc="sc")
                bankfree[pbn] = ("sc", p.cnt["sc"])
            v_qkv = p.cnt["sc"]
            maybe_stop("B", qkv[:, :], xln[:, :])

            for pr in range(2):
                dstb = qh if pr == 0 else kh
                bias_off = 24 + 8 * pr
                for hc in range(8):
                    pb, pbn = (pC, "pC") if hc % 2 == 0 else (pD, "pD")
                    slot, wv, si = ws.acquire()
                    for kc in range(8):
                        p.op("te", "matmul", pb[:, :], slot[:, kc * 128:(kc + 1) * 128],
                             qkv[:, (pr * 8 + kc) * 512:(pr * 8 + kc + 1) * 512],
                             start=(kc == 0), stop=(kc == 7),
                             waits=(wv, ("sc", v_qkv)) + (bwait(pbn) if kc == 0 else ()),
                             inc="te" if kc == 7 else None)
                    ws.release(si)
                    v_mm = p.cnt["te"]
                    for half in range(2):
                        h = 2 * hc + half
                        p.op("ve", "tensor_scalar",
                             dstb[:, h * 512:(h + 1) * 512], pb[:, :],
                             biasb[:, 88 + half:89 + half],
                             biasb[:, 90 + 16 * pr + h:91 + 16 * pr + h],
                             op0=ALU.mult, op1=ALU.add,
                             waits=(("te", v_mm), ("x", 1)),
                             inc="ve" if half == 1 else None)
                    bankfree[pbn] = ("ve", p.cnt["ve"])
            v_qkh = p.cnt["ve"]
            for tg in range(4):
                for hf in range(2):
                    i = tg * 2 + hf
                    pb, pbn = (pA, "pA") if i % 2 == 0 else (pB, "pB")
                    for kc in range(8):
                        p.op("te", "matmul", pb[:, :],
                             qkv[:, (16 + kc) * 512 + tg * 128: (16 + kc) * 512 + (tg + 1) * 128],
                             wv_hold[:, kc * 1024 + hf * 512: kc * 1024 + (hf + 1) * 512],
                             start=(kc == 0), stop=(kc == 7),
                             waits=(("y", v_wv), ("sc", v_qkv)) + (bwait(pbn) if kc == 0 else ()),
                             inc="te" if kc == 7 else None)
                    p.op("ve", "tensor_copy",
                         vh[:, tg * 1024 + hf * 512: tg * 1024 + (hf + 1) * 512],
                         pb[:, :], waits=(("te", p.cnt["te"]),), inc="ve")
                    bankfree[pbn] = ("ve", p.cnt["ve"])
            v_vh = p.cnt["ve"]
            maybe_stop("C", qh[:, :], kh[:, :], vh[:, :])

            p.op("sc", "preload_act_table", AF.Exp)
            sm3 = sm.rearrange("p (h c) -> p h c", c=128)
            for b in range(4):
                bank_end = []
                for h in range(NH):
                    bank = scoreb[h // 4]
                    w = (("ve", v_qkh),)
                    if h % 4 == 0:
                        w = w + bwait("pA pB pC pD".split()[h // 4])
                    p.op("te", "matmul", bank[:, (h % 4) * 128:(h % 4 + 1) * 128],
                         qh[:, h * 512 + b * 128: h * 512 + (b + 1) * 128],
                         kh[:, h * 512 + b * 128: h * 512 + (b + 1) * 128],
                         start=True, stop=True, waits=w,
                         inc="te" if h % 4 == 3 else None)
                    if h % 4 == 3:
                        bank_end.append(p.cnt["te"])
                if b == 0 and stop == "D0":
                    for bk in range(4):
                        p.op("ve", "tensor_copy", sm[:, bk * 512:(bk + 1) * 512],
                             scoreb[bk][:, :], waits=(("te", bank_end[bk]),), inc="ve")
                    maybe_stop("D0", sm[:, :])
                for bk in range(4):
                    p.op("ve", "scalar_tensor_tensor",
                         sm3[:, bk * 4:(bk + 1) * 4, :],
                         scoreb[bk].rearrange("p (s c) -> p s c", c=128),
                         0.125,
                         maskA.unsqueeze(1).broadcast_to([128, 4, 128]),
                         op0=ALU.mult, op1=ALU.add,
                         waits=(("te", bank_end[bk]), ("gp", v_gp), ("sc", v_exp_prev)),
                         inc="ve")
                    bankfree["pA pB pC pD".split()[bk]] = ("ve", p.cnt["ve"])
                if b == 0:
                    maybe_stop("D1", sm[:, :])
                p.op("ve", "reduce_max", snegmax[:, :], sm3[:, :, :], axis=AX.X,
                     negate=True, waits=(("ve", p.cnt["ve"]),), inc="ve")
                v_max = p.cnt["ve"]
                for h in range(NH):
                    p.op("sc", "activation", p_bf[:, h * 128:(h + 1) * 128],
                         sm[:, h * 128:(h + 1) * 128], AF.Exp,
                         bias=snegmax[:, h:h + 1], accum_out=srowsum[:, h:h + 1],
                         waits=(("ve", v_max), ("te", v_pT_prev)),
                         inc="sc" if h == NH - 1 else None)
                v_exp = p.cnt["sc"]
                v_exp_prev = v_exp
                p.op("ve", "reciprocal", srinv16[:, :], srowsum[:, :],
                     waits=(("sc", v_exp),), inc="ve")
                v_recip = p.cnt["ve"]
                for h in range(NH):
                    p.op("ve", "tensor_scalar", p_bf[:, h * 128:(h + 1) * 128],
                         p_bf[:, h * 128:(h + 1) * 128], srinv16[:, h:h + 1], None,
                         op0=ALU.mult, waits=(("ve", v_recip),),
                         inc="ve" if h == NH - 1 else None)
                v_pn = p.cnt["ve"]
                if b == 0:
                    maybe_stop("D2", p_bf[:, :], srowsum[:, :], snegmax[:, :])
                for h in range(NH):
                    pt, ptn = (pT1, "pT1") if h < 8 else (pT2, "pT2")
                    p.op("te", "transpose", pt[:, (h % 8) * 128:(h % 8 + 1) * 128],
                         p_bf[:, h * 128:(h + 1) * 128], id_b[:, :],
                         waits=(("ve", v_pn),) + ((ptb_free[ptn],) if h % 8 == 0 else ()),
                         inc="te" if h % 8 == 7 else None)
                    if h % 8 == 7:
                        v_tp = p.cnt["te"]
                        p.op("ve", "tensor_copy", spT[:, (h - 7) * 128:(h + 1) * 128],
                             pt[:, :], waits=(("te", v_tp),), inc="ve")
                        ptb_free[ptn] = ("ve", p.cnt["ve"])
                v_spt = p.cnt["ve"]
                v_pT_prev = p.cnt["te"]
                if b == 0:
                    maybe_stop("D3", spT[:, :])
                for h in range(NH):
                    pb, pbn = (pO1, "pO1") if h < 8 else (pO2, "pO2")
                    po = 64 * (h % 2)
                    col = ((h // 2) % 4) * 128
                    w = (("ve", v_spt), ("ve", v_vh))
                    if h % 8 == 0:
                        w = w + bwait(pbn)
                    p.op("te", "matmul", pb[po:po + 64, col:col + 128],
                         vh[:, b * 1024 + h * 64: b * 1024 + (h + 1) * 64],
                         spT[:, h * 128:(h + 1) * 128],
                         start=True, stop=True, tile_position=(0, po), waits=w,
                         inc="te" if h % 8 == 7 else None)
                    if h % 8 == 7:
                        v_o = p.cnt["te"]
                        for j in range(4):
                            hc = (h // 8) * 4 + j
                            p.op("sc", "activation",
                                 o_fm[:, hc * 512 + b * 128: hc * 512 + (b + 1) * 128],
                                 pb[:, j * 128:(j + 1) * 128], AF.Identity,
                                 bias=biasb[:, 40 + hc:40 + hc + 1],
                                 waits=(("te", v_o),), inc="sc" if j == 3 else None)
                        bankfree[pbn] = ("sc", p.cnt["sc"])
            v_o_all = p.cnt["sc"]
            maybe_stop("D", o_fm[:, :])

            for cc in range(8):
                pb, pbn = (pC, "pC") if cc % 2 == 0 else (pD, "pD")
                slot, wv, si = ws.acquire()
                for kc in range(8):
                    p.op("te", "matmul", pb[:, :], slot[:, kc * 128:(kc + 1) * 128],
                         o_fm[:, kc * 512:(kc + 1) * 512],
                         start=(kc == 0), stop=(kc == 7),
                         waits=(wv, ("sc", v_o_all)) + (bwait(pbn) if kc == 0 else ()),
                         inc="te" if kc == 7 else None)
                ws.release(si)
                p.op("sc", "activation", amix[:, cc * 512:(cc + 1) * 512], pb[:, :],
                     AF.Identity, bias=biasb[:, 48 + cc:48 + cc + 1],
                     waits=(("te", p.cnt["te"]),), inc="sc")
                bankfree[pbn] = ("sc", p.cnt["sc"])
            v_amix = p.cnt["sc"]
            stagef_free = [("sc", 0), ("sc", 0)]
            for dc in range(DC):
                pb, pbn = (pA, "pA") if dc % 2 == 0 else (pB, "pB")
                slot, wv, si = ws.acquire()
                for kc in range(8):
                    p.op("te", "matmul", pb[:, :], slot[:, kc * 128:(kc + 1) * 128],
                         amix[:, kc * 512:(kc + 1) * 512],
                         start=(kc == 0), stop=(kc == 7),
                         waits=(wv, ("sc", v_amix)) + (bwait(pbn) if kc == 0 else ()),
                         inc="te" if kc == 7 else None)
                ws.release(si)
                v_mm = p.cnt["te"]
                sf = stage_pf[:, (dc % 2) * 512:(dc % 2 + 1) * 512]
                p.op("sc", "activation", sf, pb[:, :], AF.Identity,
                     bias=biasb[:, 56 + dc:56 + dc + 1],
                     waits=(("te", v_mm), stagef_free[dc % 2]), inc="sc")
                v_st = p.cnt["sc"]
                bankfree[pbn] = ("sc", v_st)
                pt, ptn = (pO1, "pO1") if dc % 2 == 0 else (pO2, "pO2")
                pt3 = pt.rearrange("p (s c) -> p s c", c=128)
                for tg in range(4):
                    p.op("te", "transpose", pt3[:, tg, :], sf[:, tg * 128:(tg + 1) * 128],
                         id_f[:, :],
                         waits=(("sc", v_st),) + (bwait(ptn) if tg == 0 else ()),
                         inc="te" if tg == 3 else None)
                stagef_free[dc % 2] = ("te", p.cnt["te"])
                p.op("ve", "tensor_copy",
                     attn_out.rearrange("p (s g c) -> p s g c", s=4, c=128)[:, :, dc, :],
                     pt3[:, :, :], waits=(("te", p.cnt["te"]),), inc="ve")
                bankfree[ptn] = ("ve", p.cnt["ve"])
            v_attn_out = p.cnt["ve"]
            maybe_stop("E", amix[:, :], attn_out[:, :])

            p.op("sc", "preload_act_table", AF.Sqrt)
            v_r2 = []
            for g in range(4):
                src = attn_out[:, g * D:(g + 1) * D]
                v_rinv = ln_stats(g, src, dumpF[:, :], (("ve", v_attn_out),))
                p.op("ve", "tensor_scalar", snegmr[:, g:g + 1], smean[:, g:g + 1],
                     srinv[:, g:g + 1], -1.0, op0=ALU.mult, op1=ALU.mult,
                     waits=(("ve", v_rinv),), inc="ve")
                rg = r[:, g * D:(g + 1) * D]
                p.op("ve", "tensor_scalar", rg, rg, snegmr[:, g:g + 1], None, op0=ALU.add,
                     waits=(("ve", p.cnt["ve"]),), inc="ve")
                p.op("ve", "scalar_tensor_tensor", rg, src, srinv[:, g:g + 1], rg,
                     op0=ALU.mult, op1=ALU.add, waits=(("ve", p.cnt["ve"]),), inc="ve")
                v_r2.append(p.cnt["ve"])

            maybe_stop("F", r[:, :])

            for g in range(4):
                rg = r[:, g * D:(g + 1) * D]
                v_rinv = ln_stats(g, rg, dumpG[:, :], (("ve", v_r2[g]),))
                st = stage_tm[:, (g % 2) * D: (g % 2 + 1) * D]
                p.op("ve", "tensor_scalar", st, rg, smean[:, g:g + 1],
                     srinv[:, g:g + 1], op0=ALU.subtract, op1=ALU.mult,
                     waits=(stfree[g % 2], ("ve", v_rinv)), inc="ve")
                v_app = p.cnt["ve"]
                v_xln2 = tm_to_fm(st, xln23, g * 128, waits=(("ve", v_app),))
                stfree[g % 2] = ("te", p.cnt["te"])

            maybe_stop("G", xln2[:, :])

            p.op("sc", "preload_act_table", AF.Silu)
            for i, (off, po) in enumerate([(0, 0), (512, 32)]):
                pb, pbn = (pO1, "pO1") if i == 0 else (pO2, "pO2")
                for kc in range(DC):
                    p.op("te", "matmul", pb[po:po + RK, :],
                         lora[:, off + kc * RK: off + (kc + 1) * RK],
                         xln23[:, kc, :],
                         start=(kc == 0), stop=(kc == DC - 1),
                         waits=(("ve", v_xln2), ("x", v_lcon)) + (bwait(pbn) if kc == 0 else ()),
                         tile_position=(0, po) if po else None,
                         inc="te" if kc == DC - 1 else None)
                p.op("ve", "tensor_copy", xabuf[po:po + RK, :], pb[po:po + RK, :],
                     waits=(("te", p.cnt["te"]),), inc="ve")
                bankfree[pbn] = ("ve", p.cnt["ve"])
            v_xa = p.cnt["ve"]
            for mc in range(MCH):
                banks = [(pA, "pA"), (pC, "pC")] if mc % 2 == 0 else [(pB, "pB"), (pD, "pD")]
                for gu in range(2):
                    pb, pbn = banks[gu]
                    nt_done = 0
                    for blk in range(4):
                        slot, wv, si = ws.acquire()
                        ntile = slot.shape[-1] // 128
                        for ti in range(ntile):
                            if nt_done == 0:
                                rhs = xabuf[:, :]
                                w = (wv, ("ve", v_xa)) + bwait(pbn)
                            else:
                                rhs = xln23[:, nt_done - 1, :]
                                w = (wv,) if ti == 0 else ()
                            p.op("te", "matmul", pb[:, :],
                                 slot[:, ti * 128:(ti + 1) * 128], rhs,
                                 start=(nt_done == 0), stop=(nt_done == 32), waits=w,
                                 inc="te" if ti == ntile - 1 else None)
                            nt_done += 1
                        ws.release(si)
                v_gu = p.cnt["te"]
                pg, pgn = banks[0]
                pu, pun = banks[1]
                gs = gsil[:, (mc % 2) * 512:(mc % 2 + 1) * 512]
                p.op("sc", "activation", gs, pg[:, :], AF.Silu,
                     waits=(("te", v_gu),) + (gsil_free[mc % 2],), inc="sc")
                p.op("ve", "tensor_tensor", hidden[:, mc * 512:(mc + 1) * 512],
                     gs, pu[:, :], op=ALU.mult,
                     waits=(("sc", p.cnt["sc"]),), inc="ve")
                gsil_free[mc % 2] = ("ve", p.cnt["ve"])
                for n in (pgn, pun):
                    bankfree[n] = ("ve", p.cnt["ve"])
            v_hid = p.cnt["ve"]
            maybe_stop("H", hidden[:, :], xabuf[:, :])

            for kc in range(MCH):
                p.op("te", "matmul", pO1[0:RK, :],
                     lora[:, 1024 + kc * RK: 1024 + (kc + 1) * RK],
                     hidden[:, kc * 512:(kc + 1) * 512],
                     start=(kc == 0), stop=(kc == MCH - 1),
                     waits=(("ve", v_hid),) + (bwait("pO1") if kc == 0 else ()),
                     inc="te" if kc == MCH - 1 else None)
            p.op("ve", "tensor_copy", xabuf[0:RK, :], pO1[0:RK, :],
                 waits=(("te", p.cnt["te"]),), inc="ve")
            bankfree["pO1"] = ("ve", p.cnt["ve"])
            v_xd = p.cnt["ve"]
            stagedn_free = [("te", 0), ("te", 0)]
            for dc in range(DC):
                pb, pbn = (pA, "pA") if dc % 2 == 0 else (pB, "pB")
                nt_done = 0
                for b in range(11):
                    slot, wv, si = ws.acquire()
                    ntile = slot.shape[-1] // 128
                    for ti in range(ntile):
                        if nt_done == 0:
                            rhs = xabuf[:, :]
                            w = (wv, ("ve", v_xd)) + bwait(pbn)
                        else:
                            rhs = hidden[:, (nt_done - 1) * 512: nt_done * 512]
                            w = (wv,) if ti == 0 else ()
                        p.op("te", "matmul", pb[:, :], slot[:, ti * 128:(ti + 1) * 128],
                             rhs, start=(nt_done == 0), stop=(nt_done == 86), waits=w,
                             inc="te" if ti == ntile - 1 else None)
                        nt_done += 1
                    ws.release(si)
                v_mm = p.cnt["te"]
                sf = stage_dn[:, (dc % 2) * 512:(dc % 2 + 1) * 512]
                p.op("sc", "copy", sf, pb[:, :],
                     waits=(("te", v_mm), stagedn_free[dc % 2]), inc="sc")
                v_st = p.cnt["sc"]
                bankfree[pbn] = ("sc", v_st)
                pt, ptn = (pT1, "pT1") if dc % 2 == 0 else (pT2, "pT2")
                pt3 = pt.rearrange("p (s c) -> p s c", c=128)
                for tg in range(4):
                    p.op("te", "transpose", pt3[:, tg, :], sf[:, tg * 128:(tg + 1) * 128],
                         id_b[:, :],
                         waits=(("sc", v_st),) + ((ptb_free[ptn],) if tg == 0 else ()),
                         inc="te" if tg == 3 else None)
                stagedn_free[dc % 2] = ("te", p.cnt["te"])
                p.op("ve", "tensor_copy",
                     ffn_out.rearrange("p (s g c) -> p s g c", s=4, c=128)[:, :, dc, :],
                     pt3[:, 0:4, :], waits=(("te", p.cnt["te"]),), inc="ve")
                ptb_free[ptn] = ("ve", p.cnt["ve"])
            v_ffn_out = p.cnt["ve"]
            maybe_stop("I", ffn_out[:, :])

            p.op("sc", "preload_act_table", AF.Sqrt)
            nxt_xv = [None] * 6
            nrow0 = (t + 1) * TT
            for g in range(4):
                src = ffn_out[:, g * D:(g + 1) * D]
                col = 4 + (g % 2)
                v_rinv = ln_stats(col, src, dumpJ[:, :], (("ve", v_ffn_out),))
                p.op("ve", "tensor_scalar", snegmr[:, col:col + 1], smean[:, col:col + 1],
                     srinv[:, col:col + 1], -1.0, op0=ALU.mult, op1=ALU.mult,
                     waits=(("ve", v_rinv),), inc="ve")
                rg = r[:, g * D:(g + 1) * D]
                p.op("ve", "tensor_scalar", rg, rg, snegmr[:, col:col + 1], None, op0=ALU.add,
                     waits=(("ve", p.cnt["ve"]),), inc="ve")
                p.op("ve", "scalar_tensor_tensor", rg, src, srinv[:, col:col + 1], rg,
                     op0=ALU.mult, op1=ALU.add, waits=(("ve", p.cnt["ve"]),), inc="ve")
                p.op("sc", "dma_start", out_d[t * TT + g * 128: t * TT + (g + 1) * 128, :],
                     rg, waits=(("ve", p.cnt["ve"]),), inc=O_SEMS[g])
                if t + 1 < NT:
                    if g == 0:
                        nxt_xv[0] = ("p0", p.op("gp", "dma_start", halo[:, 0:D],
                                         xs[nrow0: nrow0 + 128, :],
                                         waits=(("te", p.cnt["te"]),),
                                         inc="p0"))
                        nxt_xv[5] = ("p5", p.op("gp", "dma_start", halo[:, D:2 * D],
                                         xs[nrow0 + 5 * 128: nrow0 + 6 * 128, :],
                                         inc="p5"))
                    ps_ = P_SEMS[g + 1]
                    nxt_xv[g + 1] = (ps_, p.op("gp", "dma_start", r[:, g * D:(g + 1) * D],
                                         xs[nrow0 + (g + 1) * 128: nrow0 + (g + 2) * 128, :],
                                         waits=((O_SEMS[g], p.cnt[O_SEMS[g]]),), inc=ps_))

        assert ws.acquired == len(ws.q), "weight stream not fully consumed"
        p.op("sy", "nop",
             waits=tuple((s, p.cnt[s]) for s in O_SEMS + P_SEMS)
             + (("x", p.cnt["x"]), ("y", p.cnt["y"])))


    try:
        _emit_tiles()
    except _Stop:
        pass

    with nc.Block() as block:
        @block.tensor
        def _(eng):
            p.replay("te", eng, sems)

        @block.vector
        def _(eng):
            p.replay("ve", eng, sems)

        @block.scalar
        def _(eng):
            p.replay("sc", eng, sems)

        @block.sync
        def _(eng):
            p.replay("sy", eng, sems)

        @block.gpsimd
        def _(eng):
            p.replay("gp", eng, sems)

    es.close()
    return nc


_NC_CACHE = None


def _get_nc():
    global _NC_CACHE
    if _NC_CACHE is None:
        _NC_CACHE = build_nc()
    return _NC_CACHE


def _prep(inputs):
    f32 = np.float32

    def a(name):
        return np.ascontiguousarray(np.asarray(inputs[name], f32))

    cw = a("L0_attn_conv_w")
    conv_s = cw.reshape(3, 32, 128, 24, 128).transpose(3, 0, 1, 2, 4)
    conv_s = np.ascontiguousarray(
        conv_s.reshape(24, 96, 128, 128).transpose(0, 2, 1, 3).reshape(24, 128, 96 * 128)
    ).astype(BF)

    def qk_lay(w):
        wf = w.reshape(1024, 1024)
        return wf.reshape(8, 128, 8, 128).transpose(2, 1, 0, 3).reshape(8, 128, 1024)

    qk_s = np.ascontiguousarray(
        np.stack([qk_lay(a("L0_attn_wq")), qk_lay(a("L0_attn_wk"))])).astype(BF)
    wv_s = np.ascontiguousarray(
        a("L0_attn_wv").reshape(8, 128, 1024).transpose(1, 0, 2).reshape(128, 8192)
    ).astype(BF)
    wo_s = np.ascontiguousarray(
        a("L0_attn_wo").reshape(8, 128, 8, 128).transpose(2, 1, 0, 3).reshape(8, 128, 1024)
    ).astype(BF)
    pj_s = np.ascontiguousarray(
        a("L0_attn_proj_w").reshape(8, 128, 32, 128).transpose(2, 1, 0, 3).reshape(32, 128, 1024)
    ).astype(BF)

    def gu_lay(w, b, row0):
        pad = np.zeros((86, 128, 128), f32)
        pad[:, row0:row0 + 16, :] = b.reshape(86, 128, 16).transpose(0, 2, 1)
        tiles = w.reshape(86, 128, 32, 128).transpose(0, 2, 3, 1)
        allt = np.concatenate([pad[:, None], tiles], axis=1)
        return allt.transpose(0, 2, 1, 3).reshape(86, 128, 33 * 128)

    gu_s = np.ascontiguousarray(np.stack([
        gu_lay(a("L0_ffn_gate_w"), a("L0_ffn_gate_b"), 0),
        gu_lay(a("L0_ffn_up_w"), a("L0_ffn_up_b"), 32),
    ], axis=1)).astype(BF)

    dwpad = np.zeros((32, 128, 128), f32)
    dwpad[:, :16, :] = a("L0_ffn_down_b").reshape(32, 128, 16).transpose(0, 2, 1)
    dtiles = a("L0_ffn_down_w").reshape(32, 128, 86, 128).transpose(0, 2, 3, 1)
    dn_s = np.ascontiguousarray(
        np.concatenate([dwpad[:, None], dtiles], axis=1)
        .transpose(0, 2, 1, 3).reshape(32, 128, 87 * 128)
    ).astype(BF)

    def aT_lay(w, nch):
        return w.reshape(16, nch, 128).transpose(2, 1, 0).reshape(128, nch * 16)

    lcon = np.ascontiguousarray(np.concatenate([
        aT_lay(a("L0_ffn_gate_a"), 32), aT_lay(a("L0_ffn_up_a"), 32),
        aT_lay(a("L0_ffn_down_a"), 86)], axis=1)).astype(BF)

    biasb = np.zeros((128, 122), f32)
    biasb[:, 0:24] = a("L0_attn_conv_b").reshape(24, 128).T
    biasb[:, 24:32] = a("L0_attn_bq").reshape(8, 128).T
    biasb[:, 32:40] = a("L0_attn_bk").reshape(8, 128).T
    biasb[:, 40:48] = a("L0_attn_bv").reshape(8, 128).T
    biasb[:, 48:56] = a("L0_attn_bo").reshape(8, 128).T
    biasb[:, 56:88] = a("L0_attn_proj_b").reshape(32, 128).T
    biasb[0:64, 88] = 1.0
    biasb[64:128, 89] = 1.0
    bqf = a("L0_attn_bq").reshape(1024)
    bkf = a("L0_attn_bk").reshape(1024)
    for h in range(16):
        po = 64 * (h % 2)
        biasb[po:po + 64, 90 + h] = bqf[h * 64:(h + 1) * 64]
        biasb[po:po + 64, 106 + h] = bkf[h * 64:(h + 1) * 64]
    biasb = np.ascontiguousarray(biasb)

    shared = dict(conv_s=conv_s, qk_s=qk_s, wv_s=wv_s, wo_s=wo_s, pj_s=pj_s,
                  gu_s=gu_s, dn_s=dn_s, lcon=lcon, biasb=biasb)

    x = a("L0_attn_inputs").reshape(8192, 4096)
    in_maps = []
    for c in range(N_CORES):
        seq = c // 4
        base = c * 1024
        xs_c = np.zeros((1280, 4096), f32)
        lo = max(seq * 4096, base - 128)
        hi = min((seq + 1) * 4096, base + 1152)
        xs_c[lo - (base - 128): hi - (base - 128)] = x[lo:hi]
        m = dict(shared)
        m["xs"] = xs_c
        in_maps.append(m)
    return in_maps


def kernel(**inputs):
    in_maps = _prep(inputs)
    nc = _get_nc()
    res = run_bass_kernel_spmd(nc, in_maps, core_ids=list(range(N_CORES)))
    out = np.concatenate([res.results[c]["out"] for c in range(N_CORES)], axis=0)
    return out.reshape(2, 4096, 4096)


# version 06
# speedup vs baseline: 1.0301x; 1.0301x over previous
import numpy as np
import ml_dtypes
from contextlib import ExitStack

import concourse.bass as bass
import concourse.mybir as mybir
from concourse.bass_utils import run_bass_kernel_spmd

F32 = mybir.dt.float32
BF16 = mybir.dt.bfloat16
U8 = mybir.dt.uint8
AF = mybir.ActivationFunctionType
ALU = mybir.AluOpType
AX = mybir.AxisListType
BF = ml_dtypes.bfloat16

D = 4096
DC = 32
WQ = 1024
CC3 = 24
NH = 16
DH = 64
HID = 11008
MCH = 86
TT = 512
NT = 2
RK = 16
NEG = -1e9
EPS = 1e-5
N_CORES = 8

RING_SLOTS = 6
RING_COLS = 1152

ENGS = ("te", "ve", "sc", "sy", "gp")
W_SEMS = tuple(f"w{i}" for i in range(RING_SLOTS))
O_SEMS = tuple(f"o{i}" for i in range(4))
P_SEMS = tuple(f"p{i}" for i in range(6))
DMA_SEMS = W_SEMS + O_SEMS + P_SEMS + ("x", "y")


class Prog:

    def __init__(self):
        self.ops = {e: [] for e in ENGS}
        self.cnt = dict(x=0, y=0, te=0, ve=0, sc=0, gp=0)
        for s in O_SEMS + P_SEMS:
            self.cnt[s] = 0
        for s in W_SEMS:
            self.cnt[s] = 0
        self.lastwait = {e: {} for e in ENGS}

    def op(self, eng, meth, *args, waits=(), inc=None, **kwargs):
        wl = []
        for sem, val in waits:
            if not val:
                continue
            if sem in DMA_SEMS:
                val = self.cnt[sem]
            if self.lastwait[eng].get(sem, 0) >= val:
                continue
            self.lastwait[eng][sem] = val
            wl.append((sem, val))
        iv = None
        if inc:
            self.cnt[inc] += 16 if inc in DMA_SEMS else 1
            iv = self.cnt[inc]
        self.ops[eng].append((tuple(wl), meth, args, kwargs, inc))
        return iv

    def replay(self, ename, eng, sems):
        for wl, meth, args, kwargs, inc in self.ops[ename]:
            for sem, val in wl:
                eng.wait_ge(sems[sem], val)
            r = getattr(eng, meth)(*args, **kwargs)
            if inc:
                r.then_inc(sems[inc], 16 if inc in DMA_SEMS else 1)


class WStream:

    def __init__(self, p, buf_ap):
        self.p = p
        self.buf = buf_ap
        self.q = []
        self.freeval = [0] * RING_SLOTS
        self.issued = 0
        self.acquired = 0
        self.wv = {}

    def push(self, src, cols):
        assert cols <= RING_COLS
        self.q.append((src, cols))

    def _issue(self):
        i = self.issued
        src, cols = self.q[i]
        s = i % RING_SLOTS
        self.wv[i] = self.p.op(
            "sy", "dma_start",
            self.buf[:, s * RING_COLS: s * RING_COLS + cols], src,
            waits=(("te", self.freeval[s]),), inc=W_SEMS[s],
        )
        self.issued += 1

    def acquire(self):
        i = self.acquired
        while self.issued < min(i + RING_SLOTS, len(self.q)) :
            self._issue()
        self.acquired += 1
        s = i % RING_SLOTS
        cols = self.q[i][1]
        return (self.buf[:, s * RING_COLS: s * RING_COLS + cols],
                (W_SEMS[s], self.wv.pop(i)), s)

    def release(self, s):
        self.freeval[s] = self.p.cnt["te"]


def build_nc(stop=None):
    nc = bass.Bass("TRN2", target_bir_lowering=False)

    xs = nc.dram_tensor("xs", [1280, D], F32, kind="ExternalInput").ap()
    conv_s = nc.dram_tensor("conv_s", [CC3, 128, 96 * 128], BF16, kind="ExternalInput").ap()
    qk_s = nc.dram_tensor("qk_s", [2, 8, 128, 1024], BF16, kind="ExternalInput").ap()
    wv_s = nc.dram_tensor("wv_s", [128, 8192], BF16, kind="ExternalInput").ap()
    wo_s = nc.dram_tensor("wo_s", [8, 128, 1024], BF16, kind="ExternalInput").ap()
    pj_s = nc.dram_tensor("pj_s", [DC, 128, 1024], BF16, kind="ExternalInput").ap()
    gu_s = nc.dram_tensor("gu_s", [MCH, 2, 128, 33 * 128], BF16, kind="ExternalInput").ap()
    dn_s = nc.dram_tensor("dn_s", [DC, 128, 87 * 128], BF16, kind="ExternalInput").ap()
    lcon = nc.dram_tensor("lcon", [128, 2400], BF16, kind="ExternalInput").ap()
    biasb_in = nc.dram_tensor("biasb", [128, 122], F32, kind="ExternalInput").ap()
    out_d = nc.dram_tensor("out", [NT * TT, D], F32, kind="ExternalOutput").ap()

    es = ExitStack()

    def sb(name, shape, dtype):
        return es.enter_context(nc.sbuf_tensor(name, shape, dtype)).ap()

    def ps(name, shape, dtype):
        return es.enter_context(nc.psum_tensor(name, shape, dtype)).ap()

    r = sb("r", [128, 4 * D], F32)
    ring_t = sb("ring", [128, RING_SLOTS * RING_COLS], BF16)
    lora = sb("lora", [128, 2400], BF16)
    xabuf = sb("xabuf", [128, 512], BF16)
    biasb = sb("biasbuf", [128, 122], F32)
    id_f = sb("id_f", [128, 128], F32)
    id_b = sb("id_b", [128, 128], BF16)
    maskA = sb("maskA", [128, 128], F32)
    stage_dn = sb("stage_dn", [128, 1024], BF16)
    gsil = sb("gsil", [128, 1024], BF16)
    ssum = sb("ssum", [128, 8], F32)
    ssq = sb("ssq", [128, 8], F32)
    smean = sb("smean", [128, 8], F32)
    svar = sb("svar", [128, 8], F32)
    smsq = sb("smsq", [128, 8], F32)
    sstd = sb("sstd", [128, 8], F32)
    srinv = sb("srinv", [128, 8], F32)
    snegmr = sb("snegmr", [128, 8], F32)
    snegmax = sb("snegmax", [128, NH], F32)
    srowsum = sb("srowsum", [128, NH], F32)
    srinv16 = sb("srinv16", [128, NH], F32)

    POOL_BYTES = 120832
    pool_t = es.enter_context(nc.sbuf_tensor("pool", [128, POOL_BYTES], U8))
    pool_base = nc.lookup_mloc(pool_t).addr
    carves = []

    def carve(name, off, shape, dtype, phases):
        nbytes = int(np.prod(shape[1:])) * mybir.dt.size(dtype)
        assert off + nbytes <= POOL_BYTES, f"{name} overflows pool"
        for (n2, o2, b2, ph2) in carves:
            if off < o2 + b2 and o2 < off + nbytes and (phases & ph2):
                raise AssertionError(f"pool overlap {name} vs {n2}")
        carves.append((name, off, nbytes, phases))
        h = nc.alloc_sbuf_tensor_at(name, shape, dtype, offset=pool_base + off)
        return h.ap()

    PH = lambda *ks: frozenset(ks)
    xln = carve("xln", 0, [128, DC * 514], BF16, PH("A", "B"))
    stage_tm = carve("stage_tm", 32896, [128, 2 * D], BF16, PH("A", "G"))
    halo = carve("halo", 49280, [128, 2 * D], F32, PH("A", "J"))
    dumpA = carve("dumpA", 82048, [128, D], BF16, PH("A"))
    qkv = carve("qkv", 32896, [128, CC3 * 512], BF16, PH("B", "C"))
    wv_hold = carve("wv_hold", 57472, [128, 8192], BF16, PH("C"))
    qh = carve("qh", 73856, [128, 16 * 512], BF16, PH("C", "D"))
    kh = carve("kh", 90240, [128, 16 * 512], BF16, PH("C", "D"))
    vh = carve("vh", 106624, [128, 4 * 1024], BF16, PH("C", "D"))
    sm = carve("sm", 0, [128, NH * 128], F32, PH("D"))
    p_bf = carve("p_bf", 8192, [128, 2 * NH * 128], BF16, PH("D"))
    spT = carve("spT", 16384, [128, 2 * NH * 128], BF16, PH("D"))
    o_fm = carve("o_fm", 24576, [128, 8 * 512], BF16, PH("D", "E"))
    amix = carve("amix", 102400, [128, 8 * 512], BF16, PH("E"))
    stage_pf = carve("stage_pf", 32768, [128, 1024], F32, PH("E"))
    attn_out = carve("attn_out", 36864, [128, 4 * D], F32, PH("E", "F"))
    dumpF = carve("dumpF", 0, [128, D], BF16, PH("F"))
    xln2 = carve("xln2", 0, [128, DC * 512], BF16, PH("G", "H"))
    dumpG = carve("dumpG", 49280, [128, D], BF16, PH("G"))
    hidden = carve("hidden", 32768, [128, MCH * 512], BF16, PH("H", "I"))
    ffn_out = carve("ffn_out", 0, [128, 4 * D], BF16, PH("I", "J"))
    dumpJ = carve("dumpJ", 32768, [128, D], BF16, PH("J"))

    xln3 = xln.rearrange("p (c t) -> p c t", t=514)
    xln23 = xln2.rearrange("p (c t) -> p c t", t=512)

    pA = ps("pA", [128, 512], F32)
    pB = ps("pB", [128, 512], F32)
    pC = ps("pC", [128, 512], F32)
    pD = ps("pD", [128, 512], F32)
    pO1 = ps("pO1", [128, 512], F32)
    pO2 = ps("pO2", [128, 512], F32)
    pT1 = ps("pT1", [128, 1024], BF16)
    pT2 = ps("pT2", [128, 1024], BF16)
    scoreb = [pA, pB, pC, pD]

    sems = {}
    for s in W_SEMS + O_SEMS + P_SEMS + ("x", "y", "te", "ve", "sc", "gp"):
        sems[s] = es.enter_context(nc.semaphore("sem_" + s))

    p = Prog()
    ws = WStream(p, ring_t)
    bankfree = {}
    stfree = {0: ("te", 0), 1: ("te", 0)}
    gsil_free = {0: ("ve", 0), 1: ("ve", 0)}
    v_pT_par = {0: 0, 1: 0}
    v_o_par = {0: 0, 1: 0}
    v_spt_par_init = {}
    v_exp_prev = 0
    v_pT_prev = 0

    def bwait(*banks):
        return tuple(bankfree[b] for b in banks if b in bankfree)

    p.op("gp", "memset", xabuf[:, :], 0.0, inc="gp")
    p.op("gp", "memset", maskA[:, :], 0.0, inc="gp")
    p.op("gp", "affine_select", out=maskA[:, :], in_=maskA[:, :],
         compare_op=ALU.is_ge, fill=NEG, base=0, pattern=[[-1, 128]],
         channel_multiplier=1, waits=(("gp", p.cnt["gp"]),), inc="gp")
    p.op("gp", "memset", id_f[:, :], 0.0, inc="gp")
    p.op("gp", "affine_select", out=id_f[:, :], in_=id_f[:, :],
         compare_op=ALU.not_equal, fill=1.0, base=0, pattern=[[-1, 128]],
         channel_multiplier=1, waits=(("gp", p.cnt["gp"]),), inc="gp")
    v_gp = p.cnt["gp"]
    p.op("ve", "tensor_copy", id_b[:, :], id_f[:, :], waits=(("gp", v_gp),), inc="ve")
    v_idb = p.cnt["ve"]
    p.op("sc", "dma_start", lora[:, :], lcon, inc="x")
    v_lcon = p.cnt["x"]
    p.op("sc", "dma_start", biasb[:, :], biasb_in, inc="x")
    v_bias = p.cnt["x"]
    p.op("sc", "preload_act_table", AF.Sqrt)

    def ln_stats(col, src, dump, waits):
        p.op("ve", "reduce_sum", ssum[:, col:col + 1], src, axis=AX.X,
             waits=waits, inc="ve")
        v_sum = p.cnt["ve"]
        p.op("sc", "activation", dump, src, AF.Square,
             accum_out=ssq[:, col:col + 1], waits=waits, inc="sc")
        v_sq = p.cnt["sc"]
        p.op("ve", "tensor_scalar", smean[:, col:col + 1], ssum[:, col:col + 1],
             1.0 / D, None, op0=ALU.mult, waits=(("ve", v_sum),), inc="ve")
        v_mean = p.cnt["ve"]
        p.op("ve", "tensor_scalar", svar[:, col:col + 1], ssq[:, col:col + 1],
             1.0 / D, EPS, op0=ALU.mult, op1=ALU.add, waits=(("sc", v_sq),))
        p.op("ve", "tensor_tensor", smsq[:, col:col + 1], smean[:, col:col + 1],
             smean[:, col:col + 1], op=ALU.mult, waits=(("ve", v_mean),), inc="ve")
        v_msq = p.cnt["ve"]
        p.op("ve", "tensor_tensor", svar[:, col:col + 1], svar[:, col:col + 1],
             smsq[:, col:col + 1], op=ALU.subtract, waits=(("ve", v_msq),), inc="ve")
        p.op("sc", "activation", sstd[:, col:col + 1], svar[:, col:col + 1],
             AF.Sqrt, waits=(("ve", p.cnt["ve"]),), inc="sc")
        p.op("ve", "reciprocal", srinv[:, col:col + 1], sstd[:, col:col + 1],
             waits=(("sc", p.cnt["sc"]),), inc="ve")
        return p.cnt["ve"]

    ptb_free = {"pT1": ("ve", 0), "pT2": ("ve", 0)}

    def tm_to_fm(src_bf, dst3, dst_col_off, halo_col=None, waits=()):
        for grp in range(4):
            pt, ptn = (pT1, "pT1") if grp % 2 == 0 else (pT2, "pT2")
            pt3 = pt.rearrange("p (c t) -> p c t", t=128)
            for j in range(8):
                dc = grp * 8 + j
                w = waits if (grp == 0 and j == 0) else ()
                p.op("te", "transpose", pt3[:, j, :], src_bf[:, dc * 128:(dc + 1) * 128],
                     id_b[:, :],
                     waits=w + (ptb_free[ptn], ("ve", v_idb)),
                     inc="te" if j == 7 else None)
            v_t = p.cnt["te"]
            dc0 = grp * 8
            if halo_col is None:
                p.op("ve", "tensor_copy",
                     dst3[:, dc0:dc0 + 8, dst_col_off:dst_col_off + 128],
                     pt3[:, 0:8, :], waits=(("te", v_t),), inc="ve")
            else:
                p.op("ve", "tensor_copy",
                     dst3[:, dc0:dc0 + 8, dst_col_off:dst_col_off + 1],
                     pt3[:, 0:8, halo_col:halo_col + 1],
                     waits=(("te", v_t),), inc="ve")
            ptb_free[ptn] = ("ve", p.cnt["ve"])
        return p.cnt["ve"]

    class _Stop(Exception):
        pass

    stopped = []

    def dbg_dump(*aps):
        for i, apx in enumerate(aps):
            d = nc.dram_tensor(f"dbg{i}", list(apx.shape), apx.dtype,
                               kind="ExternalOutput").ap()
            p.op("sc", "dma_start", d, apx,
                 waits=(("ve", p.cnt["ve"]), ("sc", p.cnt["sc"]),
                        ("te", p.cnt["te"]), ("w", p.cnt["w"]), ("gp", p.cnt["gp"])),
                 inc="x")
        p.op("sy", "nop", waits=(("x", p.cnt["x"]),))
        stopped.append(True)
        raise _Stop()

    def maybe_stop(tag, *aps):
        if stop == tag:
            dbg_dump(*aps)

    def _emit_tiles():
        nonlocal v_exp_prev, v_pT_prev
        for t in range(NT):
            row0 = t * TT

            for cc in range(CC3):
                for b in range(12):
                    ws.push(conv_s[cc, :, b * 1024:(b + 1) * 1024], 1024)
            for pr in range(2):
                for hc in range(8):
                    ws.push(qk_s[pr, hc, :, :], 1024)
            for cc in range(8):
                ws.push(wo_s[cc, :, :], 1024)
            for dc in range(DC):
                ws.push(pj_s[dc, :, :], 1024)
            for mc in range(MCH):
                for gu in range(2):
                    ws.push(gu_s[mc, gu, :, 0:1152], 1152)
                    for b in range(3):
                        ws.push(gu_s[mc, gu, :, 1152 + b * 1024: 1152 + (b + 1) * 1024], 1024)
            for dc in range(DC):
                for b in range(10):
                    ws.push(dn_s[dc, :, b * 1024:(b + 1) * 1024], 1024)
                ws.push(dn_s[dc, :, 10240:11136], 896)

            if t == 0:
                xv = []
                for g in range(6):
                    if g == 0:
                        dst = halo[:, 0:D]
                    elif g == 5:
                        dst = halo[:, D:2 * D]
                    else:
                        dst = r[:, (g - 1) * D: g * D]
                    xv.append(("x", p.op("sc", "dma_start", dst,
                                   xs[row0 + g * 128: row0 + (g + 1) * 128, :],
                                   waits=(("x", p.cnt["x"]), ("gp", v_gp)), inc="x")))
            else:
                xv = nxt_xv
            v_xln = 0
            for g in [0, 5, 1, 2, 3, 4]:
                if g == 0:
                    src = halo[:, 0:D]
                elif g == 5:
                    src = halo[:, D:2 * D]
                else:
                    src = r[:, (g - 1) * D: g * D]
                wv = (xv[g],)
                v_rinv = ln_stats(g, src, dumpA[:, :], wv)
                st = stage_tm[:, (g % 2) * D: (g % 2 + 1) * D]
                p.op("ve", "tensor_scalar", st, src, smean[:, g:g + 1],
                     srinv[:, g:g + 1], op0=ALU.subtract, op1=ALU.mult,
                     waits=(stfree[g % 2], ("ve", v_rinv)), inc="ve")
                v_app = p.cnt["ve"]
                if g == 0:
                    v_xln = tm_to_fm(st, xln3, 0, halo_col=127, waits=(("ve", v_app),))
                elif g == 5:
                    v_xln = tm_to_fm(st, xln3, 513, halo_col=0, waits=(("ve", v_app),))
                else:
                    v_xln = tm_to_fm(st, xln3, 1 + (g - 1) * 128, waits=(("ve", v_app),))
                stfree[g % 2] = ("te", p.cnt["te"])
            v_halo_done = v_xln
            maybe_stop("A", xln[:, :], r[:, :])

            p.op("sy", "dma_start", wv_hold[:, 0:4096], wv_s[:, 0:4096],
                 waits=(("ve", v_halo_done),), inc="y")
            p.op("sy", "dma_start", wv_hold[:, 4096:8192], wv_s[:, 4096:8192], inc="y")
            v_wv = p.cnt["y"]

            for cc in range(CC3):
                pb, pbn = (pA, "pA") if cc % 2 == 0 else (pB, "pB")
                first_w = bwait(pbn) + (("ve", v_xln),)
                for b in range(12):
                    slot, wv, si = ws.acquire()
                    for ti in range(8):
                        tt_i = b * 8 + ti
                        k, dc = tt_i // 32, tt_i % 32
                        p.op("te", "matmul", pb[:, :], slot[:, ti * 128:(ti + 1) * 128],
                             xln3[:, dc, k:k + TT],
                             start=(tt_i == 0), stop=(tt_i == 95),
                             waits=(wv,) + (first_w if tt_i == 0 else ()),
                             inc="te" if ti == 7 else None)
                    ws.release(si)
                v_mm = p.cnt["te"]
                p.op("sc", "activation", qkv[:, cc * 512:(cc + 1) * 512], pb[:, :],
                     AF.Identity, bias=biasb[:, cc:cc + 1],
                     waits=(("te", v_mm), ("x", v_bias)), inc="sc")
                bankfree[pbn] = ("sc", p.cnt["sc"])
            v_qkv = p.cnt["sc"]
            maybe_stop("B", qkv[:, :], xln[:, :])

            for pr in range(2):
                dstb = qh if pr == 0 else kh
                bias_off = 24 + 8 * pr
                for hc in range(8):
                    pb, pbn = (pC, "pC") if hc % 2 == 0 else (pD, "pD")
                    slot, wv, si = ws.acquire()
                    for kc in range(8):
                        p.op("te", "matmul", pb[:, :], slot[:, kc * 128:(kc + 1) * 128],
                             qkv[:, (pr * 8 + kc) * 512:(pr * 8 + kc + 1) * 512],
                             start=(kc == 0), stop=(kc == 7),
                             waits=(wv, ("sc", v_qkv)) + (bwait(pbn) if kc == 0 else ()),
                             inc="te" if kc == 7 else None)
                    ws.release(si)
                    v_mm = p.cnt["te"]
                    for half in range(2):
                        h = 2 * hc + half
                        p.op("ve", "tensor_scalar",
                             dstb[:, h * 512:(h + 1) * 512], pb[:, :],
                             biasb[:, 88 + half:89 + half],
                             biasb[:, 90 + 16 * pr + h:91 + 16 * pr + h],
                             op0=ALU.mult, op1=ALU.add,
                             waits=(("te", v_mm), ("x", 1)),
                             inc="ve" if half == 1 else None)
                    bankfree[pbn] = ("ve", p.cnt["ve"])
            v_qkh = p.cnt["ve"]
            for tg in range(4):
                for hf in range(2):
                    i = tg * 2 + hf
                    pb, pbn = (pA, "pA") if i % 2 == 0 else (pB, "pB")
                    for kc in range(8):
                        p.op("te", "matmul", pb[:, :],
                             qkv[:, (16 + kc) * 512 + tg * 128: (16 + kc) * 512 + (tg + 1) * 128],
                             wv_hold[:, kc * 1024 + hf * 512: kc * 1024 + (hf + 1) * 512],
                             start=(kc == 0), stop=(kc == 7),
                             waits=(("y", v_wv), ("sc", v_qkv)) + (bwait(pbn) if kc == 0 else ()),
                             inc="te" if kc == 7 else None)
                    p.op("ve", "tensor_copy",
                         vh[:, tg * 1024 + hf * 512: tg * 1024 + (hf + 1) * 512],
                         pb[:, :], waits=(("te", p.cnt["te"]),), inc="ve")
                    bankfree[pbn] = ("ve", p.cnt["ve"])
            v_vh = p.cnt["ve"]
            maybe_stop("C", qh[:, :], kh[:, :], vh[:, :])

            p.op("sc", "preload_act_table", AF.Exp)
            sm3 = sm.rearrange("p (h c) -> p h c", c=128)
            v_norm = {}
            v_spt_par = dict(v_spt_par_init)
            for it in range(5):
                if it < 4:
                    b = it
                    pbuf = p_bf[:, (b % 2) * 2048:(b % 2 + 1) * 2048]
                    bank_end = []
                    for h in range(NH):
                        bank = scoreb[h // 4]
                        w = (("ve", v_qkh),)
                        if h % 4 == 0:
                            w = w + bwait("pA pB pC pD".split()[h // 4])
                        p.op("te", "matmul", bank[:, (h % 4) * 128:(h % 4 + 1) * 128],
                             qh[:, h * 512 + b * 128: h * 512 + (b + 1) * 128],
                             kh[:, h * 512 + b * 128: h * 512 + (b + 1) * 128],
                             start=True, stop=True, waits=w,
                             inc="te" if h % 4 == 3 else None)
                        if h % 4 == 3:
                            bank_end.append(p.cnt["te"])
                        if h in (7, 15):
                            for bk in ((0, 1) if h == 7 else (2, 3)):
                                p.op("ve", "scalar_tensor_tensor",
                                     sm3[:, bk * 4:(bk + 1) * 4, :],
                                     scoreb[bk].rearrange("p (s c) -> p s c", c=128),
                                     0.125,
                                     maskA.unsqueeze(1).broadcast_to([128, 4, 128]),
                                     op0=ALU.mult, op1=ALU.add,
                                     waits=(("te", bank_end[bk]), ("gp", v_gp),
                                            ("sc", v_exp_prev)),
                                     inc="ve")
                                bankfree["pA pB pC pD".split()[bk]] = ("ve", p.cnt["ve"])
                    if b == 0:
                        maybe_stop("D1", sm[:, :])
                    p.op("ve", "reduce_max", snegmax[:, :], sm3[:, :, :], axis=AX.X,
                         negate=True, waits=(("ve", p.cnt["ve"]),), inc="ve")
                    v_max = p.cnt["ve"]
                    for h in range(NH):
                        p.op("sc", "activation", pbuf[:, h * 128:(h + 1) * 128],
                             sm[:, h * 128:(h + 1) * 128], AF.Exp,
                             bias=snegmax[:, h:h + 1], accum_out=srowsum[:, h:h + 1],
                             waits=(("ve", v_max), ("te", v_pT_par[b % 2])),
                             inc="sc" if h == NH - 1 else None)
                    v_exp_prev = p.cnt["sc"]
                    p.op("ve", "reciprocal", srinv16[:, :], srowsum[:, :],
                         waits=(("sc", v_exp_prev),), inc="ve")
                    v_recip = p.cnt["ve"]
                    for h in range(NH):
                        p.op("ve", "tensor_scalar", pbuf[:, h * 128:(h + 1) * 128],
                             pbuf[:, h * 128:(h + 1) * 128], srinv16[:, h:h + 1], None,
                             op0=ALU.mult, waits=(("ve", v_recip),),
                             inc="ve" if h == NH - 1 else None)
                    v_norm[b] = p.cnt["ve"]
                if it >= 1:
                    b = it - 1
                    pbuf = p_bf[:, (b % 2) * 2048:(b % 2 + 1) * 2048]
                    spt = spT[:, (b % 2) * 2048:(b % 2 + 1) * 2048]
                    for h in range(NH):
                        pt, ptn = (pT1, "pT1") if h < 8 else (pT2, "pT2")
                        p.op("te", "transpose", pt[:, (h % 8) * 128:(h % 8 + 1) * 128],
                             pbuf[:, h * 128:(h + 1) * 128], id_b[:, :],
                             waits=(("ve", v_norm[b]),) + ((ptb_free[ptn],) if h % 8 == 0 else ()),
                             inc="te" if h % 8 == 7 else None)
                        if h % 8 == 7:
                            v_tp = p.cnt["te"]
                            p.op("ve", "tensor_copy", spt[:, (h - 7) * 128:(h + 1) * 128],
                                 pt[:, :],
                                 waits=(("te", v_tp), ("te", v_o_par[b % 2])), inc="ve")
                            ptb_free[ptn] = ("ve", p.cnt["ve"])
                    v_spt_b = p.cnt["ve"]
                    v_pT_par[b % 2] = p.cnt["te"]
                    for h in range(NH):
                        pb, pbn = (pO1, "pO1") if h < 8 else (pO2, "pO2")
                        po = 64 * (h % 2)
                        col = ((h // 2) % 4) * 128
                        w = (("ve", v_spt_b), ("ve", v_vh))
                        if h % 8 == 0:
                            w = w + bwait(pbn)
                        p.op("te", "matmul", pb[po:po + 64, col:col + 128],
                             vh[:, b * 1024 + h * 64: b * 1024 + (h + 1) * 64],
                             spt[:, h * 128:(h + 1) * 128],
                             start=True, stop=True, tile_position=(0, po), waits=w,
                             inc="te" if h % 8 == 7 else None)
                        if h % 8 == 7:
                            v_o = p.cnt["te"]
                            for j in range(4):
                                hc = (h // 8) * 4 + j
                                p.op("sc", "activation",
                                     o_fm[:, hc * 512 + b * 128: hc * 512 + (b + 1) * 128],
                                     pb[:, j * 128:(j + 1) * 128], AF.Identity,
                                     bias=biasb[:, 40 + hc:40 + hc + 1],
                                     waits=(("te", v_o),), inc="sc" if j == 3 else None)
                            bankfree[pbn] = ("sc", p.cnt["sc"])
                    v_o_par[b % 2] = p.cnt["te"]
            v_o_all = p.cnt["sc"]
            maybe_stop("D", o_fm[:, :])

            for cc in range(8):
                pb, pbn = (pC, "pC") if cc % 2 == 0 else (pD, "pD")
                slot, wv, si = ws.acquire()
                for kc in range(8):
                    p.op("te", "matmul", pb[:, :], slot[:, kc * 128:(kc + 1) * 128],
                         o_fm[:, kc * 512:(kc + 1) * 512],
                         start=(kc == 0), stop=(kc == 7),
                         waits=(wv, ("sc", v_o_all)) + (bwait(pbn) if kc == 0 else ()),
                         inc="te" if kc == 7 else None)
                ws.release(si)
                p.op("sc", "activation", amix[:, cc * 512:(cc + 1) * 512], pb[:, :],
                     AF.Identity, bias=biasb[:, 48 + cc:48 + cc + 1],
                     waits=(("te", p.cnt["te"]),), inc="sc")
                bankfree[pbn] = ("sc", p.cnt["sc"])
            v_amix = p.cnt["sc"]
            stagef_free = [("sc", 0), ("sc", 0)]
            for dc in range(DC):
                pb, pbn = (pA, "pA") if dc % 2 == 0 else (pB, "pB")
                slot, wv, si = ws.acquire()
                for kc in range(8):
                    p.op("te", "matmul", pb[:, :], slot[:, kc * 128:(kc + 1) * 128],
                         amix[:, kc * 512:(kc + 1) * 512],
                         start=(kc == 0), stop=(kc == 7),
                         waits=(wv, ("sc", v_amix)) + (bwait(pbn) if kc == 0 else ()),
                         inc="te" if kc == 7 else None)
                ws.release(si)
                v_mm = p.cnt["te"]
                sf = stage_pf[:, (dc % 2) * 512:(dc % 2 + 1) * 512]
                p.op("sc", "activation", sf, pb[:, :], AF.Identity,
                     bias=biasb[:, 56 + dc:56 + dc + 1],
                     waits=(("te", v_mm), stagef_free[dc % 2]), inc="sc")
                v_st = p.cnt["sc"]
                bankfree[pbn] = ("sc", v_st)
                pt, ptn = (pO1, "pO1") if dc % 2 == 0 else (pO2, "pO2")
                pt3 = pt.rearrange("p (s c) -> p s c", c=128)
                for tg in range(4):
                    p.op("te", "transpose", pt3[:, tg, :], sf[:, tg * 128:(tg + 1) * 128],
                         id_f[:, :],
                         waits=(("sc", v_st),) + (bwait(ptn) if tg == 0 else ()),
                         inc="te" if tg == 3 else None)
                stagef_free[dc % 2] = ("te", p.cnt["te"])
                p.op("ve", "tensor_copy",
                     attn_out.rearrange("p (s g c) -> p s g c", s=4, c=128)[:, :, dc, :],
                     pt3[:, :, :], waits=(("te", p.cnt["te"]),), inc="ve")
                bankfree[ptn] = ("ve", p.cnt["ve"])
            v_attn_out = p.cnt["ve"]
            maybe_stop("E", amix[:, :], attn_out[:, :])

            p.op("sc", "preload_act_table", AF.Sqrt)
            v_r2 = []
            for g in range(4):
                src = attn_out[:, g * D:(g + 1) * D]
                v_rinv = ln_stats(g, src, dumpF[:, :], (("ve", v_attn_out),))
                p.op("ve", "tensor_scalar", snegmr[:, g:g + 1], smean[:, g:g + 1],
                     srinv[:, g:g + 1], -1.0, op0=ALU.mult, op1=ALU.mult,
                     waits=(("ve", v_rinv),), inc="ve")
                rg = r[:, g * D:(g + 1) * D]
                p.op("ve", "tensor_scalar", rg, rg, snegmr[:, g:g + 1], None, op0=ALU.add,
                     waits=(("ve", p.cnt["ve"]),), inc="ve")
                p.op("ve", "scalar_tensor_tensor", rg, src, srinv[:, g:g + 1], rg,
                     op0=ALU.mult, op1=ALU.add, waits=(("ve", p.cnt["ve"]),), inc="ve")
                v_r2.append(p.cnt["ve"])

            maybe_stop("F", r[:, :])

            for g in range(4):
                rg = r[:, g * D:(g + 1) * D]
                v_rinv = ln_stats(g, rg, dumpG[:, :], (("ve", v_r2[g]),))
                st = stage_tm[:, (g % 2) * D: (g % 2 + 1) * D]
                p.op("ve", "tensor_scalar", st, rg, smean[:, g:g + 1],
                     srinv[:, g:g + 1], op0=ALU.subtract, op1=ALU.mult,
                     waits=(stfree[g % 2], ("ve", v_rinv)), inc="ve")
                v_app = p.cnt["ve"]
                v_xln2 = tm_to_fm(st, xln23, g * 128, waits=(("ve", v_app),))
                stfree[g % 2] = ("te", p.cnt["te"])

            maybe_stop("G", xln2[:, :])

            p.op("sc", "preload_act_table", AF.Silu)
            for i, (off, po) in enumerate([(0, 0), (512, 32)]):
                pb, pbn = (pO1, "pO1") if i == 0 else (pO2, "pO2")
                for kc in range(DC):
                    p.op("te", "matmul", pb[po:po + RK, :],
                         lora[:, off + kc * RK: off + (kc + 1) * RK],
                         xln23[:, kc, :],
                         start=(kc == 0), stop=(kc == DC - 1),
                         waits=(("ve", v_xln2), ("x", v_lcon)) + (bwait(pbn) if kc == 0 else ()),
                         tile_position=(0, po) if po else None,
                         inc="te" if kc == DC - 1 else None)
                p.op("ve", "tensor_copy", xabuf[po:po + RK, :], pb[po:po + RK, :],
                     waits=(("te", p.cnt["te"]),), inc="ve")
                bankfree[pbn] = ("ve", p.cnt["ve"])
            v_xa = p.cnt["ve"]
            for mc in range(MCH):
                banks = [(pA, "pA"), (pC, "pC")] if mc % 2 == 0 else [(pB, "pB"), (pD, "pD")]
                for gu in range(2):
                    pb, pbn = banks[gu]
                    nt_done = 0
                    for blk in range(4):
                        slot, wv, si = ws.acquire()
                        ntile = slot.shape[-1] // 128
                        for ti in range(ntile):
                            if nt_done == 0:
                                rhs = xabuf[:, :]
                                w = (wv, ("ve", v_xa)) + bwait(pbn)
                            else:
                                rhs = xln23[:, nt_done - 1, :]
                                w = (wv,) if ti == 0 else ()
                            p.op("te", "matmul", pb[:, :],
                                 slot[:, ti * 128:(ti + 1) * 128], rhs,
                                 start=(nt_done == 0), stop=(nt_done == 32), waits=w,
                                 inc="te" if ti == ntile - 1 else None)
                            nt_done += 1
                        ws.release(si)
                v_gu = p.cnt["te"]
                pg, pgn = banks[0]
                pu, pun = banks[1]
                gs = gsil[:, (mc % 2) * 512:(mc % 2 + 1) * 512]
                p.op("sc", "activation", gs, pg[:, :], AF.Silu,
                     waits=(("te", v_gu),) + (gsil_free[mc % 2],), inc="sc")
                p.op("ve", "tensor_tensor", hidden[:, mc * 512:(mc + 1) * 512],
                     gs, pu[:, :], op=ALU.mult,
                     waits=(("sc", p.cnt["sc"]),), inc="ve")
                gsil_free[mc % 2] = ("ve", p.cnt["ve"])
                for n in (pgn, pun):
                    bankfree[n] = ("ve", p.cnt["ve"])
            v_hid = p.cnt["ve"]
            maybe_stop("H", hidden[:, :], xabuf[:, :])

            for kc in range(MCH):
                p.op("te", "matmul", pO1[0:RK, :],
                     lora[:, 1024 + kc * RK: 1024 + (kc + 1) * RK],
                     hidden[:, kc * 512:(kc + 1) * 512],
                     start=(kc == 0), stop=(kc == MCH - 1),
                     waits=(("ve", v_hid),) + (bwait("pO1") if kc == 0 else ()),
                     inc="te" if kc == MCH - 1 else None)
            p.op("ve", "tensor_copy", xabuf[0:RK, :], pO1[0:RK, :],
                 waits=(("te", p.cnt["te"]),), inc="ve")
            bankfree["pO1"] = ("ve", p.cnt["ve"])
            v_xd = p.cnt["ve"]
            stagedn_free = [("te", 0), ("te", 0)]
            for dc in range(DC):
                pb, pbn = (pA, "pA") if dc % 2 == 0 else (pB, "pB")
                nt_done = 0
                for b in range(11):
                    slot, wv, si = ws.acquire()
                    ntile = slot.shape[-1] // 128
                    for ti in range(ntile):
                        if nt_done == 0:
                            rhs = xabuf[:, :]
                            w = (wv, ("ve", v_xd)) + bwait(pbn)
                        else:
                            rhs = hidden[:, (nt_done - 1) * 512: nt_done * 512]
                            w = (wv,) if ti == 0 else ()
                        p.op("te", "matmul", pb[:, :], slot[:, ti * 128:(ti + 1) * 128],
                             rhs, start=(nt_done == 0), stop=(nt_done == 86), waits=w,
                             inc="te" if ti == ntile - 1 else None)
                        nt_done += 1
                    ws.release(si)
                v_mm = p.cnt["te"]
                sf = stage_dn[:, (dc % 2) * 512:(dc % 2 + 1) * 512]
                p.op("sc", "copy", sf, pb[:, :],
                     waits=(("te", v_mm), stagedn_free[dc % 2]), inc="sc")
                v_st = p.cnt["sc"]
                bankfree[pbn] = ("sc", v_st)
                pt, ptn = (pT1, "pT1") if dc % 2 == 0 else (pT2, "pT2")
                pt3 = pt.rearrange("p (s c) -> p s c", c=128)
                for tg in range(4):
                    p.op("te", "transpose", pt3[:, tg, :], sf[:, tg * 128:(tg + 1) * 128],
                         id_b[:, :],
                         waits=(("sc", v_st),) + ((ptb_free[ptn],) if tg == 0 else ()),
                         inc="te" if tg == 3 else None)
                stagedn_free[dc % 2] = ("te", p.cnt["te"])
                p.op("ve", "tensor_copy",
                     ffn_out.rearrange("p (s g c) -> p s g c", s=4, c=128)[:, :, dc, :],
                     pt3[:, 0:4, :], waits=(("te", p.cnt["te"]),), inc="ve")
                ptb_free[ptn] = ("ve", p.cnt["ve"])
            v_ffn_out = p.cnt["ve"]
            maybe_stop("I", ffn_out[:, :])

            p.op("sc", "preload_act_table", AF.Sqrt)
            nxt_xv = [None] * 6
            nrow0 = (t + 1) * TT
            for g in range(4):
                src = ffn_out[:, g * D:(g + 1) * D]
                col = 4 + (g % 2)
                v_rinv = ln_stats(col, src, dumpJ[:, :], (("ve", v_ffn_out),))
                p.op("ve", "tensor_scalar", snegmr[:, col:col + 1], smean[:, col:col + 1],
                     srinv[:, col:col + 1], -1.0, op0=ALU.mult, op1=ALU.mult,
                     waits=(("ve", v_rinv),), inc="ve")
                rg = r[:, g * D:(g + 1) * D]
                p.op("ve", "tensor_scalar", rg, rg, snegmr[:, col:col + 1], None, op0=ALU.add,
                     waits=(("ve", p.cnt["ve"]),), inc="ve")
                p.op("ve", "scalar_tensor_tensor", rg, src, srinv[:, col:col + 1], rg,
                     op0=ALU.mult, op1=ALU.add, waits=(("ve", p.cnt["ve"]),), inc="ve")
                p.op("sc", "dma_start", out_d[t * TT + g * 128: t * TT + (g + 1) * 128, :],
                     rg, waits=(("ve", p.cnt["ve"]),), inc=O_SEMS[g])
                if t + 1 < NT:
                    if g == 0:
                        nxt_xv[0] = ("p0", p.op("gp", "dma_start", halo[:, 0:D],
                                         xs[nrow0: nrow0 + 128, :],
                                         waits=(("te", p.cnt["te"]),),
                                         inc="p0"))
                        nxt_xv[5] = ("p5", p.op("gp", "dma_start", halo[:, D:2 * D],
                                         xs[nrow0 + 5 * 128: nrow0 + 6 * 128, :],
                                         inc="p5"))
                    ps_ = P_SEMS[g + 1]
                    nxt_xv[g + 1] = (ps_, p.op("gp", "dma_start", r[:, g * D:(g + 1) * D],
                                         xs[nrow0 + (g + 1) * 128: nrow0 + (g + 2) * 128, :],
                                         waits=((O_SEMS[g], p.cnt[O_SEMS[g]]),), inc=ps_))

        assert ws.acquired == len(ws.q), "weight stream not fully consumed"
        p.op("sy", "nop",
             waits=tuple((s, p.cnt[s]) for s in O_SEMS + P_SEMS)
             + (("x", p.cnt["x"]), ("y", p.cnt["y"])))


    try:
        _emit_tiles()
    except _Stop:
        pass

    with nc.Block() as block:
        @block.tensor
        def _(eng):
            p.replay("te", eng, sems)

        @block.vector
        def _(eng):
            p.replay("ve", eng, sems)

        @block.scalar
        def _(eng):
            p.replay("sc", eng, sems)

        @block.sync
        def _(eng):
            p.replay("sy", eng, sems)

        @block.gpsimd
        def _(eng):
            p.replay("gp", eng, sems)

    es.close()
    return nc


_NC_CACHE = None


def _get_nc():
    global _NC_CACHE
    if _NC_CACHE is None:
        _NC_CACHE = build_nc()
    return _NC_CACHE


def _prep(inputs):
    f32 = np.float32

    def a(name):
        return np.ascontiguousarray(np.asarray(inputs[name], f32))

    cw = a("L0_attn_conv_w")
    conv_s = cw.reshape(3, 32, 128, 24, 128).transpose(3, 0, 1, 2, 4)
    conv_s = np.ascontiguousarray(
        conv_s.reshape(24, 96, 128, 128).transpose(0, 2, 1, 3).reshape(24, 128, 96 * 128)
    ).astype(BF)

    def qk_lay(w):
        wf = w.reshape(1024, 1024)
        return wf.reshape(8, 128, 8, 128).transpose(2, 1, 0, 3).reshape(8, 128, 1024)

    qk_s = np.ascontiguousarray(
        np.stack([qk_lay(a("L0_attn_wq")), qk_lay(a("L0_attn_wk"))])).astype(BF)
    wv_s = np.ascontiguousarray(
        a("L0_attn_wv").reshape(8, 128, 1024).transpose(1, 0, 2).reshape(128, 8192)
    ).astype(BF)
    wo_s = np.ascontiguousarray(
        a("L0_attn_wo").reshape(8, 128, 8, 128).transpose(2, 1, 0, 3).reshape(8, 128, 1024)
    ).astype(BF)
    pj_s = np.ascontiguousarray(
        a("L0_attn_proj_w").reshape(8, 128, 32, 128).transpose(2, 1, 0, 3).reshape(32, 128, 1024)
    ).astype(BF)

    def gu_lay(w, b, row0):
        pad = np.zeros((86, 128, 128), f32)
        pad[:, row0:row0 + 16, :] = b.reshape(86, 128, 16).transpose(0, 2, 1)
        tiles = w.reshape(86, 128, 32, 128).transpose(0, 2, 3, 1)
        allt = np.concatenate([pad[:, None], tiles], axis=1)
        return allt.transpose(0, 2, 1, 3).reshape(86, 128, 33 * 128)

    gu_s = np.ascontiguousarray(np.stack([
        gu_lay(a("L0_ffn_gate_w"), a("L0_ffn_gate_b"), 0),
        gu_lay(a("L0_ffn_up_w"), a("L0_ffn_up_b"), 32),
    ], axis=1)).astype(BF)

    dwpad = np.zeros((32, 128, 128), f32)
    dwpad[:, :16, :] = a("L0_ffn_down_b").reshape(32, 128, 16).transpose(0, 2, 1)
    dtiles = a("L0_ffn_down_w").reshape(32, 128, 86, 128).transpose(0, 2, 3, 1)
    dn_s = np.ascontiguousarray(
        np.concatenate([dwpad[:, None], dtiles], axis=1)
        .transpose(0, 2, 1, 3).reshape(32, 128, 87 * 128)
    ).astype(BF)

    def aT_lay(w, nch):
        return w.reshape(16, nch, 128).transpose(2, 1, 0).reshape(128, nch * 16)

    lcon = np.ascontiguousarray(np.concatenate([
        aT_lay(a("L0_ffn_gate_a"), 32), aT_lay(a("L0_ffn_up_a"), 32),
        aT_lay(a("L0_ffn_down_a"), 86)], axis=1)).astype(BF)

    biasb = np.zeros((128, 122), f32)
    biasb[:, 0:24] = a("L0_attn_conv_b").reshape(24, 128).T
    biasb[:, 24:32] = a("L0_attn_bq").reshape(8, 128).T
    biasb[:, 32:40] = a("L0_attn_bk").reshape(8, 128).T
    biasb[:, 40:48] = a("L0_attn_bv").reshape(8, 128).T
    biasb[:, 48:56] = a("L0_attn_bo").reshape(8, 128).T
    biasb[:, 56:88] = a("L0_attn_proj_b").reshape(32, 128).T
    biasb[0:64, 88] = 1.0
    biasb[64:128, 89] = 1.0
    bqf = a("L0_attn_bq").reshape(1024)
    bkf = a("L0_attn_bk").reshape(1024)
    for h in range(16):
        po = 64 * (h % 2)
        biasb[po:po + 64, 90 + h] = bqf[h * 64:(h + 1) * 64]
        biasb[po:po + 64, 106 + h] = bkf[h * 64:(h + 1) * 64]
    biasb = np.ascontiguousarray(biasb)

    shared = dict(conv_s=conv_s, qk_s=qk_s, wv_s=wv_s, wo_s=wo_s, pj_s=pj_s,
                  gu_s=gu_s, dn_s=dn_s, lcon=lcon, biasb=biasb)

    x = a("L0_attn_inputs").reshape(8192, 4096)
    in_maps = []
    for c in range(N_CORES):
        seq = c // 4
        base = c * 1024
        xs_c = np.zeros((1280, 4096), f32)
        lo = max(seq * 4096, base - 128)
        hi = min((seq + 1) * 4096, base + 1152)
        xs_c[lo - (base - 128): hi - (base - 128)] = x[lo:hi]
        m = dict(shared)
        m["xs"] = xs_c
        in_maps.append(m)
    return in_maps


def kernel(**inputs):
    in_maps = _prep(inputs)
    nc = _get_nc()
    res = run_bass_kernel_spmd(nc, in_maps, core_ids=list(range(N_CORES)))
    out = np.concatenate([res.results[c]["out"] for c in range(N_CORES)], axis=0)
    return out.reshape(2, 4096, 4096)


# version 07
# speedup vs baseline: 1.1397x; 1.1063x over previous
import numpy as np
import ml_dtypes
from contextlib import ExitStack

import concourse.bass as bass
import concourse.mybir as mybir
from concourse.bass_utils import run_bass_kernel_spmd

F32 = mybir.dt.float32
BF16 = mybir.dt.bfloat16
U8 = mybir.dt.uint8
AF = mybir.ActivationFunctionType
ALU = mybir.AluOpType
AX = mybir.AxisListType
BF = ml_dtypes.bfloat16

D = 4096
DC = 32
WQ = 1024
CC3 = 24
NH = 16
DH = 64
HID = 11008
MCH = 86
TT = 512
NT = 2
RK = 16
NEG = -1e9
EPS = 1e-5
N_CORES = 8

RING_SLOTS = 6
RING_COLS = 1152

ENGS = ("te", "ve", "sc", "sy", "gp")
W_SEMS = tuple(f"w{i}" for i in range(RING_SLOTS))
O_SEMS = tuple(f"o{i}" for i in range(4))
P_SEMS = tuple(f"p{i}" for i in range(6))
DMA_SEMS = W_SEMS + O_SEMS + P_SEMS + ("x", "y")


class Prog:

    def __init__(self):
        self.ops = {e: [] for e in ENGS}
        self.cnt = dict(x=0, y=0, te=0, ve=0, sc=0, gp=0)
        for s in O_SEMS + P_SEMS:
            self.cnt[s] = 0
        for s in W_SEMS:
            self.cnt[s] = 0
        self.lastwait = {e: {} for e in ENGS}

    def op(self, eng, meth, *args, waits=(), inc=None, **kwargs):
        wl = []
        for sem, val in waits:
            if not val:
                continue
            if sem in DMA_SEMS:
                val = self.cnt[sem]
            if self.lastwait[eng].get(sem, 0) >= val:
                continue
            self.lastwait[eng][sem] = val
            wl.append((sem, val))
        iv = None
        if inc:
            self.cnt[inc] += 16 if inc in DMA_SEMS else 1
            iv = self.cnt[inc]
        self.ops[eng].append((tuple(wl), meth, args, kwargs, inc))
        return iv

    def replay(self, ename, eng, sems):
        for wl, meth, args, kwargs, inc in self.ops[ename]:
            for sem, val in wl:
                eng.wait_ge(sems[sem], val)
            r = getattr(eng, meth)(*args, **kwargs)
            if inc:
                r.then_inc(sems[inc], 16 if inc in DMA_SEMS else 1)


class WStream:

    def __init__(self, p, buf_ap):
        self.p = p
        self.buf = buf_ap
        self.q = []
        self.freeval = [0] * RING_SLOTS
        self.issued = 0
        self.acquired = 0
        self.wv = {}

    def push(self, src, cols):
        assert cols <= RING_COLS
        self.q.append((src, cols))

    def _issue(self):
        i = self.issued
        src, cols = self.q[i]
        s = i % RING_SLOTS
        self.wv[i] = self.p.op(
            "gp", "dma_start",
            self.buf[:, s * RING_COLS: s * RING_COLS + cols], src,
            waits=(("te", self.freeval[s]),), inc=W_SEMS[s],
        )
        self.issued += 1

    def acquire(self):
        i = self.acquired
        while self.issued < min(i + RING_SLOTS, len(self.q)) :
            self._issue()
        self.acquired += 1
        s = i % RING_SLOTS
        cols = self.q[i][1]
        return (self.buf[:, s * RING_COLS: s * RING_COLS + cols],
                (W_SEMS[s], self.wv.pop(i)), s)

    def release(self, s):
        self.freeval[s] = self.p.cnt["te"]


def build_nc(stop=None):
    nc = bass.Bass("TRN2", target_bir_lowering=False)

    xs = nc.dram_tensor("xs", [1280, D], F32, kind="ExternalInput").ap()
    conv_s = nc.dram_tensor("conv_s", [CC3, 128, 96 * 128], F32, kind="ExternalInput").ap()
    qk_s = nc.dram_tensor("qk_s", [2, 8, 128, 1024], F32, kind="ExternalInput").ap()
    wv_s = nc.dram_tensor("wv_s", [128, 8192], F32, kind="ExternalInput").ap()
    wo_s = nc.dram_tensor("wo_s", [8, 128, 1024], F32, kind="ExternalInput").ap()
    pj_s = nc.dram_tensor("pj_s", [DC, 128, 1024], F32, kind="ExternalInput").ap()
    gu_s = nc.dram_tensor("gu_s", [MCH, 2, 128, 33 * 128], F32, kind="ExternalInput").ap()
    dn_s = nc.dram_tensor("dn_s", [DC, 128, 87 * 128], F32, kind="ExternalInput").ap()
    lcon = nc.dram_tensor("lcon", [128, 2400], F32, kind="ExternalInput").ap()
    biasb_in = nc.dram_tensor("biasb", [128, 122], F32, kind="ExternalInput").ap()
    out_d = nc.dram_tensor("out", [NT * TT, D], F32, kind="ExternalOutput").ap()

    es = ExitStack()

    def sb(name, shape, dtype):
        return es.enter_context(nc.sbuf_tensor(name, shape, dtype)).ap()

    def ps(name, shape, dtype):
        return es.enter_context(nc.psum_tensor(name, shape, dtype)).ap()

    r = sb("r", [128, 4 * D], F32)
    ring_t = sb("ring", [128, RING_SLOTS * RING_COLS], BF16)
    lora = sb("lora", [128, 2400], BF16)
    xabuf = sb("xabuf", [128, 512], BF16)
    biasb = sb("biasbuf", [128, 122], F32)
    id_f = sb("id_f", [128, 128], F32)
    id_b = sb("id_b", [128, 128], BF16)
    maskA = sb("maskA", [128, 128], F32)
    stage_dn = sb("stage_dn", [128, 1024], BF16)
    gsil = sb("gsil", [128, 1024], BF16)
    ssum = sb("ssum", [128, 8], F32)
    ssq = sb("ssq", [128, 8], F32)
    smean = sb("smean", [128, 8], F32)
    svar = sb("svar", [128, 8], F32)
    smsq = sb("smsq", [128, 8], F32)
    sstd = sb("sstd", [128, 8], F32)
    srinv = sb("srinv", [128, 8], F32)
    snegmr = sb("snegmr", [128, 8], F32)
    snegmax = sb("snegmax", [128, NH], F32)
    srowsum = sb("srowsum", [128, NH], F32)
    srinv16 = sb("srinv16", [128, NH], F32)

    POOL_BYTES = 120832
    pool_t = es.enter_context(nc.sbuf_tensor("pool", [128, POOL_BYTES], U8))
    pool_base = nc.lookup_mloc(pool_t).addr
    carves = []

    def carve(name, off, shape, dtype, phases):
        nbytes = int(np.prod(shape[1:])) * mybir.dt.size(dtype)
        assert off + nbytes <= POOL_BYTES, f"{name} overflows pool"
        for (n2, o2, b2, ph2) in carves:
            if off < o2 + b2 and o2 < off + nbytes and (phases & ph2):
                raise AssertionError(f"pool overlap {name} vs {n2}")
        carves.append((name, off, nbytes, phases))
        h = nc.alloc_sbuf_tensor_at(name, shape, dtype, offset=pool_base + off)
        return h.ap()

    PH = lambda *ks: frozenset(ks)
    xln = carve("xln", 0, [128, DC * 514], BF16, PH("A", "B"))
    stage_tm = carve("stage_tm", 32896, [128, 2 * D], BF16, PH("A", "G"))
    halo = carve("halo", 49280, [128, 2 * D], F32, PH("A", "J"))
    dumpA = carve("dumpA", 82048, [128, D], BF16, PH("A"))
    qkv = carve("qkv", 32896, [128, CC3 * 512], BF16, PH("B", "C"))
    wv_hold = carve("wv_hold", 57472, [128, 8192], BF16, PH("C"))
    qh = carve("qh", 73856, [128, 16 * 512], BF16, PH("C", "D"))
    kh = carve("kh", 90240, [128, 16 * 512], BF16, PH("C", "D"))
    vh = carve("vh", 106624, [128, 4 * 1024], BF16, PH("C", "D"))
    sm = carve("sm", 0, [128, NH * 128], F32, PH("D"))
    p_bf = carve("p_bf", 8192, [128, 2 * NH * 128], BF16, PH("D"))
    spT = carve("spT", 16384, [128, 2 * NH * 128], BF16, PH("D"))
    o_fm = carve("o_fm", 24576, [128, 8 * 512], BF16, PH("D", "E"))
    amix = carve("amix", 102400, [128, 8 * 512], BF16, PH("E"))
    stage_pf = carve("stage_pf", 32768, [128, 1024], F32, PH("E"))
    attn_out = carve("attn_out", 36864, [128, 4 * D], F32, PH("E", "F"))
    dumpF = carve("dumpF", 0, [128, D], BF16, PH("F"))
    xln2 = carve("xln2", 0, [128, DC * 512], BF16, PH("G", "H"))
    dumpG = carve("dumpG", 49280, [128, D], BF16, PH("G"))
    hidden = carve("hidden", 32768, [128, MCH * 512], BF16, PH("H", "I"))
    ffn_out = carve("ffn_out", 0, [128, 4 * D], BF16, PH("I", "J"))
    dumpJ = carve("dumpJ", 32768, [128, D], BF16, PH("J"))

    xln3 = xln.rearrange("p (c t) -> p c t", t=514)
    xln23 = xln2.rearrange("p (c t) -> p c t", t=512)

    pA = ps("pA", [128, 512], F32)
    pB = ps("pB", [128, 512], F32)
    pC = ps("pC", [128, 512], F32)
    pD = ps("pD", [128, 512], F32)
    pO1 = ps("pO1", [128, 512], F32)
    pO2 = ps("pO2", [128, 512], F32)
    pT1 = ps("pT1", [128, 1024], BF16)
    pT2 = ps("pT2", [128, 1024], BF16)
    scoreb = [pA, pB, pC, pD]

    sems = {}
    for s in W_SEMS + O_SEMS + P_SEMS + ("x", "y", "te", "ve", "sc", "gp"):
        sems[s] = es.enter_context(nc.semaphore("sem_" + s))

    p = Prog()
    ws = WStream(p, ring_t)
    bankfree = {}
    stfree = {0: ("te", 0), 1: ("te", 0)}
    gsil_free = {0: ("ve", 0), 1: ("ve", 0)}
    v_pT_par = {0: 0, 1: 0}
    v_o_par = {0: 0, 1: 0}
    v_spt_par_init = {}
    v_exp_prev = 0
    v_pT_prev = 0

    def bwait(*banks):
        return tuple(bankfree[b] for b in banks if b in bankfree)

    p.op("gp", "memset", xabuf[:, :], 0.0, inc="gp")
    p.op("gp", "memset", maskA[:, :], 0.0, inc="gp")
    p.op("gp", "affine_select", out=maskA[:, :], in_=maskA[:, :],
         compare_op=ALU.is_ge, fill=NEG, base=0, pattern=[[-1, 128]],
         channel_multiplier=1, waits=(("gp", p.cnt["gp"]),), inc="gp")
    p.op("gp", "memset", id_f[:, :], 0.0, inc="gp")
    p.op("gp", "affine_select", out=id_f[:, :], in_=id_f[:, :],
         compare_op=ALU.not_equal, fill=1.0, base=0, pattern=[[-1, 128]],
         channel_multiplier=1, waits=(("gp", p.cnt["gp"]),), inc="gp")
    v_gp = p.cnt["gp"]
    p.op("ve", "tensor_copy", id_b[:, :], id_f[:, :], waits=(("gp", v_gp),), inc="ve")
    v_idb = p.cnt["ve"]
    p.op("gp", "dma_start", lora[:, :], lcon, inc="y")
    v_lcon = p.cnt["y"]
    p.op("sc", "dma_start", biasb[:, :], biasb_in, inc="x")
    v_bias = p.cnt["x"]
    p.op("sc", "preload_act_table", AF.Sqrt)

    def ln_stats(col, src, dump, waits):
        p.op("ve", "reduce_sum", ssum[:, col:col + 1], src, axis=AX.X,
             waits=waits, inc="ve")
        v_sum = p.cnt["ve"]
        p.op("sc", "activation", dump, src, AF.Square,
             accum_out=ssq[:, col:col + 1], waits=waits, inc="sc")
        v_sq = p.cnt["sc"]
        p.op("ve", "tensor_scalar", smean[:, col:col + 1], ssum[:, col:col + 1],
             1.0 / D, None, op0=ALU.mult, waits=(("ve", v_sum),), inc="ve")
        v_mean = p.cnt["ve"]
        p.op("ve", "tensor_scalar", svar[:, col:col + 1], ssq[:, col:col + 1],
             1.0 / D, EPS, op0=ALU.mult, op1=ALU.add, waits=(("sc", v_sq),))
        p.op("ve", "tensor_tensor", smsq[:, col:col + 1], smean[:, col:col + 1],
             smean[:, col:col + 1], op=ALU.mult, waits=(("ve", v_mean),), inc="ve")
        v_msq = p.cnt["ve"]
        p.op("ve", "tensor_tensor", svar[:, col:col + 1], svar[:, col:col + 1],
             smsq[:, col:col + 1], op=ALU.subtract, waits=(("ve", v_msq),), inc="ve")
        p.op("sc", "activation", sstd[:, col:col + 1], svar[:, col:col + 1],
             AF.Sqrt, waits=(("ve", p.cnt["ve"]),), inc="sc")
        p.op("ve", "reciprocal", srinv[:, col:col + 1], sstd[:, col:col + 1],
             waits=(("sc", p.cnt["sc"]),), inc="ve")
        return p.cnt["ve"]

    ptb_free = {"pT1": ("ve", 0), "pT2": ("ve", 0)}

    def tm_to_fm(src_bf, dst3, dst_col_off, halo_col=None, waits=()):
        for grp in range(4):
            pt, ptn = (pT1, "pT1") if grp % 2 == 0 else (pT2, "pT2")
            pt3 = pt.rearrange("p (c t) -> p c t", t=128)
            for j in range(8):
                dc = grp * 8 + j
                w = waits if (grp == 0 and j == 0) else ()
                p.op("te", "transpose", pt3[:, j, :], src_bf[:, dc * 128:(dc + 1) * 128],
                     id_b[:, :],
                     waits=w + (ptb_free[ptn], ("ve", v_idb)),
                     inc="te" if j == 7 else None)
            v_t = p.cnt["te"]
            dc0 = grp * 8
            if halo_col is None:
                p.op("ve", "tensor_copy",
                     dst3[:, dc0:dc0 + 8, dst_col_off:dst_col_off + 128],
                     pt3[:, 0:8, :], waits=(("te", v_t),), inc="ve")
            else:
                p.op("ve", "tensor_copy",
                     dst3[:, dc0:dc0 + 8, dst_col_off:dst_col_off + 1],
                     pt3[:, 0:8, halo_col:halo_col + 1],
                     waits=(("te", v_t),), inc="ve")
            ptb_free[ptn] = ("ve", p.cnt["ve"])
        return p.cnt["ve"]

    class _Stop(Exception):
        pass

    stopped = []

    def dbg_dump(*aps):
        for i, apx in enumerate(aps):
            d = nc.dram_tensor(f"dbg{i}", list(apx.shape), apx.dtype,
                               kind="ExternalOutput").ap()
            p.op("sc", "dma_start", d, apx,
                 waits=(("ve", p.cnt["ve"]), ("sc", p.cnt["sc"]),
                        ("te", p.cnt["te"]), ("w", p.cnt["w"]), ("gp", p.cnt["gp"])),
                 inc="x")
        p.op("sy", "nop", waits=(("x", p.cnt["x"]),))
        stopped.append(True)
        raise _Stop()

    def maybe_stop(tag, *aps):
        if stop == tag:
            dbg_dump(*aps)

    def _emit_tiles():
        nonlocal v_exp_prev, v_pT_prev
        for t in range(NT):
            row0 = t * TT

            for cc in range(CC3):
                for b in range(12):
                    ws.push(conv_s[cc, :, b * 1024:(b + 1) * 1024], 1024)
            for pr in range(2):
                for hc in range(8):
                    ws.push(qk_s[pr, hc, :, :], 1024)
            for cc in range(8):
                ws.push(wo_s[cc, :, :], 1024)
            for dc in range(DC):
                ws.push(pj_s[dc, :, :], 1024)
            for mc in range(MCH):
                for gu in range(2):
                    ws.push(gu_s[mc, gu, :, 0:1152], 1152)
                    for b in range(3):
                        ws.push(gu_s[mc, gu, :, 1152 + b * 1024: 1152 + (b + 1) * 1024], 1024)
            for dc in range(DC):
                for b in range(10):
                    ws.push(dn_s[dc, :, b * 1024:(b + 1) * 1024], 1024)
                ws.push(dn_s[dc, :, 10240:11136], 896)

            if t == 0:
                xv = []
                for g in range(6):
                    if g == 0:
                        dst = halo[:, 0:D]
                    elif g == 5:
                        dst = halo[:, D:2 * D]
                    else:
                        dst = r[:, (g - 1) * D: g * D]
                    xv.append(("x", p.op("sc", "dma_start", dst,
                                   xs[row0 + g * 128: row0 + (g + 1) * 128, :],
                                   waits=(("x", p.cnt["x"]), ("gp", v_gp)), inc="x")))
            else:
                xv = nxt_xv
            v_xln = 0
            for g in [0, 5, 1, 2, 3, 4]:
                if g == 0:
                    src = halo[:, 0:D]
                elif g == 5:
                    src = halo[:, D:2 * D]
                else:
                    src = r[:, (g - 1) * D: g * D]
                wv = (xv[g],)
                v_rinv = ln_stats(g, src, dumpA[:, :], wv)
                st = stage_tm[:, (g % 2) * D: (g % 2 + 1) * D]
                p.op("ve", "tensor_scalar", st, src, smean[:, g:g + 1],
                     srinv[:, g:g + 1], op0=ALU.subtract, op1=ALU.mult,
                     waits=(stfree[g % 2], ("ve", v_rinv)), inc="ve")
                v_app = p.cnt["ve"]
                if g == 0:
                    v_xln = tm_to_fm(st, xln3, 0, halo_col=127, waits=(("ve", v_app),))
                elif g == 5:
                    v_xln = tm_to_fm(st, xln3, 513, halo_col=0, waits=(("ve", v_app),))
                else:
                    v_xln = tm_to_fm(st, xln3, 1 + (g - 1) * 128, waits=(("ve", v_app),))
                stfree[g % 2] = ("te", p.cnt["te"])
            v_halo_done = v_xln
            maybe_stop("A", xln[:, :], r[:, :])

            p.op("gp", "dma_start", wv_hold[:, 0:4096], wv_s[:, 0:4096],
                 waits=(("ve", v_halo_done),), inc="y")
            p.op("gp", "dma_start", wv_hold[:, 4096:8192], wv_s[:, 4096:8192], inc="y")
            v_wv = p.cnt["y"]

            for cc in range(CC3):
                pb, pbn = (pA, "pA") if cc % 2 == 0 else (pB, "pB")
                first_w = bwait(pbn) + (("ve", v_xln),)
                for b in range(12):
                    slot, wv, si = ws.acquire()
                    for ti in range(8):
                        tt_i = b * 8 + ti
                        k, dc = tt_i // 32, tt_i % 32
                        p.op("te", "matmul", pb[:, :], slot[:, ti * 128:(ti + 1) * 128],
                             xln3[:, dc, k:k + TT],
                             start=(tt_i == 0), stop=(tt_i == 95),
                             waits=(wv,) + (first_w if tt_i == 0 else ()),
                             inc="te" if ti == 7 else None)
                    ws.release(si)
                v_mm = p.cnt["te"]
                p.op("sc", "activation", qkv[:, cc * 512:(cc + 1) * 512], pb[:, :],
                     AF.Identity, bias=biasb[:, cc:cc + 1],
                     waits=(("te", v_mm), ("x", v_bias)), inc="sc")
                bankfree[pbn] = ("sc", p.cnt["sc"])
            v_qkv = p.cnt["sc"]
            maybe_stop("B", qkv[:, :], xln[:, :])

            for pr in range(2):
                dstb = qh if pr == 0 else kh
                bias_off = 24 + 8 * pr
                for hc in range(8):
                    pb, pbn = (pC, "pC") if hc % 2 == 0 else (pD, "pD")
                    slot, wv, si = ws.acquire()
                    for kc in range(8):
                        p.op("te", "matmul", pb[:, :], slot[:, kc * 128:(kc + 1) * 128],
                             qkv[:, (pr * 8 + kc) * 512:(pr * 8 + kc + 1) * 512],
                             start=(kc == 0), stop=(kc == 7),
                             waits=(wv, ("sc", v_qkv)) + (bwait(pbn) if kc == 0 else ()),
                             inc="te" if kc == 7 else None)
                    ws.release(si)
                    v_mm = p.cnt["te"]
                    for half in range(2):
                        h = 2 * hc + half
                        p.op("ve", "tensor_scalar",
                             dstb[:, h * 512:(h + 1) * 512], pb[:, :],
                             biasb[:, 88 + half:89 + half],
                             biasb[:, 90 + 16 * pr + h:91 + 16 * pr + h],
                             op0=ALU.mult, op1=ALU.add,
                             waits=(("te", v_mm), ("x", 1)),
                             inc="ve" if half == 1 else None)
                    bankfree[pbn] = ("ve", p.cnt["ve"])
            v_qkh = p.cnt["ve"]
            for tg in range(4):
                for hf in range(2):
                    i = tg * 2 + hf
                    pb, pbn = (pA, "pA") if i % 2 == 0 else (pB, "pB")
                    for kc in range(8):
                        p.op("te", "matmul", pb[:, :],
                             qkv[:, (16 + kc) * 512 + tg * 128: (16 + kc) * 512 + (tg + 1) * 128],
                             wv_hold[:, kc * 1024 + hf * 512: kc * 1024 + (hf + 1) * 512],
                             start=(kc == 0), stop=(kc == 7),
                             waits=(("y", v_wv), ("sc", v_qkv)) + (bwait(pbn) if kc == 0 else ()),
                             inc="te" if kc == 7 else None)
                    p.op("ve", "tensor_copy",
                         vh[:, tg * 1024 + hf * 512: tg * 1024 + (hf + 1) * 512],
                         pb[:, :], waits=(("te", p.cnt["te"]),), inc="ve")
                    bankfree[pbn] = ("ve", p.cnt["ve"])
            v_vh = p.cnt["ve"]
            maybe_stop("C", qh[:, :], kh[:, :], vh[:, :])

            p.op("sc", "preload_act_table", AF.Exp)
            sm3 = sm.rearrange("p (h c) -> p h c", c=128)
            v_norm = {}
            v_spt_par = dict(v_spt_par_init)
            for it in range(5):
                if it < 4:
                    b = it
                    pbuf = p_bf[:, (b % 2) * 2048:(b % 2 + 1) * 2048]
                    bank_end = []
                    for h in range(NH):
                        bank = scoreb[h // 4]
                        w = (("ve", v_qkh),)
                        if h % 4 == 0:
                            w = w + bwait("pA pB pC pD".split()[h // 4])
                        p.op("te", "matmul", bank[:, (h % 4) * 128:(h % 4 + 1) * 128],
                             qh[:, h * 512 + b * 128: h * 512 + (b + 1) * 128],
                             kh[:, h * 512 + b * 128: h * 512 + (b + 1) * 128],
                             start=True, stop=True, waits=w,
                             inc="te" if h % 4 == 3 else None)
                        if h % 4 == 3:
                            bank_end.append(p.cnt["te"])
                        if h in (7, 15):
                            for bk in ((0, 1) if h == 7 else (2, 3)):
                                p.op("ve", "scalar_tensor_tensor",
                                     sm3[:, bk * 4:(bk + 1) * 4, :],
                                     scoreb[bk].rearrange("p (s c) -> p s c", c=128),
                                     0.125,
                                     maskA.unsqueeze(1).broadcast_to([128, 4, 128]),
                                     op0=ALU.mult, op1=ALU.add,
                                     waits=(("te", bank_end[bk]), ("gp", v_gp),
                                            ("sc", v_exp_prev)),
                                     inc="ve")
                                bankfree["pA pB pC pD".split()[bk]] = ("ve", p.cnt["ve"])
                    if b == 0:
                        maybe_stop("D1", sm[:, :])
                    p.op("ve", "reduce_max", snegmax[:, :], sm3[:, :, :], axis=AX.X,
                         negate=True, waits=(("ve", p.cnt["ve"]),), inc="ve")
                    v_max = p.cnt["ve"]
                    for h in range(NH):
                        p.op("sc", "activation", pbuf[:, h * 128:(h + 1) * 128],
                             sm[:, h * 128:(h + 1) * 128], AF.Exp,
                             bias=snegmax[:, h:h + 1], accum_out=srowsum[:, h:h + 1],
                             waits=(("ve", v_max), ("te", v_pT_par[b % 2])),
                             inc="sc" if h == NH - 1 else None)
                    v_exp_prev = p.cnt["sc"]
                    p.op("ve", "reciprocal", srinv16[:, :], srowsum[:, :],
                         waits=(("sc", v_exp_prev),), inc="ve")
                    v_recip = p.cnt["ve"]
                    for h in range(NH):
                        p.op("ve", "tensor_scalar", pbuf[:, h * 128:(h + 1) * 128],
                             pbuf[:, h * 128:(h + 1) * 128], srinv16[:, h:h + 1], None,
                             op0=ALU.mult, waits=(("ve", v_recip),),
                             inc="ve" if h == NH - 1 else None)
                    v_norm[b] = p.cnt["ve"]
                if it >= 1:
                    b = it - 1
                    pbuf = p_bf[:, (b % 2) * 2048:(b % 2 + 1) * 2048]
                    spt = spT[:, (b % 2) * 2048:(b % 2 + 1) * 2048]
                    for h in range(NH):
                        pt, ptn = (pT1, "pT1") if h < 8 else (pT2, "pT2")
                        p.op("te", "transpose", pt[:, (h % 8) * 128:(h % 8 + 1) * 128],
                             pbuf[:, h * 128:(h + 1) * 128], id_b[:, :],
                             waits=(("ve", v_norm[b]),) + ((ptb_free[ptn],) if h % 8 == 0 else ()),
                             inc="te" if h % 8 == 7 else None)
                        if h % 8 == 7:
                            v_tp = p.cnt["te"]
                            p.op("ve", "tensor_copy", spt[:, (h - 7) * 128:(h + 1) * 128],
                                 pt[:, :],
                                 waits=(("te", v_tp), ("te", v_o_par[b % 2])), inc="ve")
                            ptb_free[ptn] = ("ve", p.cnt["ve"])
                    v_spt_b = p.cnt["ve"]
                    v_pT_par[b % 2] = p.cnt["te"]
                    for h in range(NH):
                        pb, pbn = (pO1, "pO1") if h < 8 else (pO2, "pO2")
                        po = 64 * (h % 2)
                        col = ((h // 2) % 4) * 128
                        w = (("ve", v_spt_b), ("ve", v_vh))
                        if h % 8 == 0:
                            w = w + bwait(pbn)
                        p.op("te", "matmul", pb[po:po + 64, col:col + 128],
                             vh[:, b * 1024 + h * 64: b * 1024 + (h + 1) * 64],
                             spt[:, h * 128:(h + 1) * 128],
                             start=True, stop=True, tile_position=(0, po), waits=w,
                             inc="te" if h % 8 == 7 else None)
                        if h % 8 == 7:
                            v_o = p.cnt["te"]
                            for j in range(4):
                                hc = (h // 8) * 4 + j
                                p.op("sc", "activation",
                                     o_fm[:, hc * 512 + b * 128: hc * 512 + (b + 1) * 128],
                                     pb[:, j * 128:(j + 1) * 128], AF.Identity,
                                     bias=biasb[:, 40 + hc:40 + hc + 1],
                                     waits=(("te", v_o),), inc="sc" if j == 3 else None)
                            bankfree[pbn] = ("sc", p.cnt["sc"])
                    v_o_par[b % 2] = p.cnt["te"]
            v_o_all = p.cnt["sc"]
            maybe_stop("D", o_fm[:, :])

            for cc in range(8):
                pb, pbn = (pC, "pC") if cc % 2 == 0 else (pD, "pD")
                slot, wv, si = ws.acquire()
                for kc in range(8):
                    p.op("te", "matmul", pb[:, :], slot[:, kc * 128:(kc + 1) * 128],
                         o_fm[:, kc * 512:(kc + 1) * 512],
                         start=(kc == 0), stop=(kc == 7),
                         waits=(wv, ("sc", v_o_all)) + (bwait(pbn) if kc == 0 else ()),
                         inc="te" if kc == 7 else None)
                ws.release(si)
                p.op("sc", "activation", amix[:, cc * 512:(cc + 1) * 512], pb[:, :],
                     AF.Identity, bias=biasb[:, 48 + cc:48 + cc + 1],
                     waits=(("te", p.cnt["te"]),), inc="sc")
                bankfree[pbn] = ("sc", p.cnt["sc"])
            v_amix = p.cnt["sc"]
            stagef_free = [("sc", 0), ("sc", 0)]
            for dc in range(DC):
                pb, pbn = (pA, "pA") if dc % 2 == 0 else (pB, "pB")
                slot, wv, si = ws.acquire()
                for kc in range(8):
                    p.op("te", "matmul", pb[:, :], slot[:, kc * 128:(kc + 1) * 128],
                         amix[:, kc * 512:(kc + 1) * 512],
                         start=(kc == 0), stop=(kc == 7),
                         waits=(wv, ("sc", v_amix)) + (bwait(pbn) if kc == 0 else ()),
                         inc="te" if kc == 7 else None)
                ws.release(si)
                v_mm = p.cnt["te"]
                sf = stage_pf[:, (dc % 2) * 512:(dc % 2 + 1) * 512]
                p.op("sc", "activation", sf, pb[:, :], AF.Identity,
                     bias=biasb[:, 56 + dc:56 + dc + 1],
                     waits=(("te", v_mm), stagef_free[dc % 2]), inc="sc")
                v_st = p.cnt["sc"]
                bankfree[pbn] = ("sc", v_st)
                pt, ptn = (pO1, "pO1") if dc % 2 == 0 else (pO2, "pO2")
                pt3 = pt.rearrange("p (s c) -> p s c", c=128)
                for tg in range(4):
                    p.op("te", "transpose", pt3[:, tg, :], sf[:, tg * 128:(tg + 1) * 128],
                         id_f[:, :],
                         waits=(("sc", v_st),) + (bwait(ptn) if tg == 0 else ()),
                         inc="te" if tg == 3 else None)
                stagef_free[dc % 2] = ("te", p.cnt["te"])
                p.op("ve", "tensor_copy",
                     attn_out.rearrange("p (s g c) -> p s g c", s=4, c=128)[:, :, dc, :],
                     pt3[:, :, :], waits=(("te", p.cnt["te"]),), inc="ve")
                bankfree[ptn] = ("ve", p.cnt["ve"])
            v_attn_out = p.cnt["ve"]
            maybe_stop("E", amix[:, :], attn_out[:, :])

            p.op("sc", "preload_act_table", AF.Sqrt)
            v_r2 = []
            for g in range(4):
                src = attn_out[:, g * D:(g + 1) * D]
                v_rinv = ln_stats(g, src, dumpF[:, :], (("ve", v_attn_out),))
                p.op("ve", "tensor_scalar", snegmr[:, g:g + 1], smean[:, g:g + 1],
                     srinv[:, g:g + 1], -1.0, op0=ALU.mult, op1=ALU.mult,
                     waits=(("ve", v_rinv),), inc="ve")
                rg = r[:, g * D:(g + 1) * D]
                p.op("ve", "tensor_scalar", rg, rg, snegmr[:, g:g + 1], None, op0=ALU.add,
                     waits=(("ve", p.cnt["ve"]),), inc="ve")
                p.op("ve", "scalar_tensor_tensor", rg, src, srinv[:, g:g + 1], rg,
                     op0=ALU.mult, op1=ALU.add, waits=(("ve", p.cnt["ve"]),), inc="ve")
                v_r2.append(p.cnt["ve"])

            maybe_stop("F", r[:, :])

            for g in range(4):
                rg = r[:, g * D:(g + 1) * D]
                v_rinv = ln_stats(g, rg, dumpG[:, :], (("ve", v_r2[g]),))
                st = stage_tm[:, (g % 2) * D: (g % 2 + 1) * D]
                p.op("ve", "tensor_scalar", st, rg, smean[:, g:g + 1],
                     srinv[:, g:g + 1], op0=ALU.subtract, op1=ALU.mult,
                     waits=(stfree[g % 2], ("ve", v_rinv)), inc="ve")
                v_app = p.cnt["ve"]
                v_xln2 = tm_to_fm(st, xln23, g * 128, waits=(("ve", v_app),))
                stfree[g % 2] = ("te", p.cnt["te"])

            maybe_stop("G", xln2[:, :])

            p.op("sc", "preload_act_table", AF.Silu)
            for i, (off, po) in enumerate([(0, 0), (512, 32)]):
                pb, pbn = (pO1, "pO1") if i == 0 else (pO2, "pO2")
                for kc in range(DC):
                    p.op("te", "matmul", pb[po:po + RK, :],
                         lora[:, off + kc * RK: off + (kc + 1) * RK],
                         xln23[:, kc, :],
                         start=(kc == 0), stop=(kc == DC - 1),
                         waits=(("ve", v_xln2), ("y", v_lcon)) + (bwait(pbn) if kc == 0 else ()),
                         tile_position=(0, po) if po else None,
                         inc="te" if kc == DC - 1 else None)
                p.op("ve", "tensor_copy", xabuf[po:po + RK, :], pb[po:po + RK, :],
                     waits=(("te", p.cnt["te"]),), inc="ve")
                bankfree[pbn] = ("ve", p.cnt["ve"])
            v_xa = p.cnt["ve"]
            for mc in range(MCH):
                banks = [(pA, "pA"), (pC, "pC")] if mc % 2 == 0 else [(pB, "pB"), (pD, "pD")]
                for gu in range(2):
                    pb, pbn = banks[gu]
                    nt_done = 0
                    for blk in range(4):
                        slot, wv, si = ws.acquire()
                        ntile = slot.shape[-1] // 128
                        for ti in range(ntile):
                            if nt_done == 0:
                                rhs = xabuf[:, :]
                                w = (wv, ("ve", v_xa)) + bwait(pbn)
                            else:
                                rhs = xln23[:, nt_done - 1, :]
                                w = (wv,) if ti == 0 else ()
                            p.op("te", "matmul", pb[:, :],
                                 slot[:, ti * 128:(ti + 1) * 128], rhs,
                                 start=(nt_done == 0), stop=(nt_done == 32), waits=w,
                                 inc="te" if ti == ntile - 1 else None)
                            nt_done += 1
                        ws.release(si)
                v_gu = p.cnt["te"]
                pg, pgn = banks[0]
                pu, pun = banks[1]
                gs = gsil[:, (mc % 2) * 512:(mc % 2 + 1) * 512]
                p.op("sc", "activation", gs, pg[:, :], AF.Silu,
                     waits=(("te", v_gu),) + (gsil_free[mc % 2],), inc="sc")
                p.op("ve", "tensor_tensor", hidden[:, mc * 512:(mc + 1) * 512],
                     gs, pu[:, :], op=ALU.mult,
                     waits=(("sc", p.cnt["sc"]),), inc="ve")
                gsil_free[mc % 2] = ("ve", p.cnt["ve"])
                for n in (pgn, pun):
                    bankfree[n] = ("ve", p.cnt["ve"])
            v_hid = p.cnt["ve"]
            maybe_stop("H", hidden[:, :], xabuf[:, :])

            for kc in range(MCH):
                p.op("te", "matmul", pO1[0:RK, :],
                     lora[:, 1024 + kc * RK: 1024 + (kc + 1) * RK],
                     hidden[:, kc * 512:(kc + 1) * 512],
                     start=(kc == 0), stop=(kc == MCH - 1),
                     waits=(("ve", v_hid),) + (bwait("pO1") if kc == 0 else ()),
                     inc="te" if kc == MCH - 1 else None)
            p.op("ve", "tensor_copy", xabuf[0:RK, :], pO1[0:RK, :],
                 waits=(("te", p.cnt["te"]),), inc="ve")
            bankfree["pO1"] = ("ve", p.cnt["ve"])
            v_xd = p.cnt["ve"]
            stagedn_free = [("te", 0), ("te", 0)]
            for dc in range(DC):
                pb, pbn = (pA, "pA") if dc % 2 == 0 else (pB, "pB")
                nt_done = 0
                for b in range(11):
                    slot, wv, si = ws.acquire()
                    ntile = slot.shape[-1] // 128
                    for ti in range(ntile):
                        if nt_done == 0:
                            rhs = xabuf[:, :]
                            w = (wv, ("ve", v_xd)) + bwait(pbn)
                        else:
                            rhs = hidden[:, (nt_done - 1) * 512: nt_done * 512]
                            w = (wv,) if ti == 0 else ()
                        p.op("te", "matmul", pb[:, :], slot[:, ti * 128:(ti + 1) * 128],
                             rhs, start=(nt_done == 0), stop=(nt_done == 86), waits=w,
                             inc="te" if ti == ntile - 1 else None)
                        nt_done += 1
                    ws.release(si)
                v_mm = p.cnt["te"]
                sf = stage_dn[:, (dc % 2) * 512:(dc % 2 + 1) * 512]
                p.op("sc", "copy", sf, pb[:, :],
                     waits=(("te", v_mm), stagedn_free[dc % 2]), inc="sc")
                v_st = p.cnt["sc"]
                bankfree[pbn] = ("sc", v_st)
                pt, ptn = (pT1, "pT1") if dc % 2 == 0 else (pT2, "pT2")
                pt3 = pt.rearrange("p (s c) -> p s c", c=128)
                for tg in range(4):
                    p.op("te", "transpose", pt3[:, tg, :], sf[:, tg * 128:(tg + 1) * 128],
                         id_b[:, :],
                         waits=(("sc", v_st),) + ((ptb_free[ptn],) if tg == 0 else ()),
                         inc="te" if tg == 3 else None)
                stagedn_free[dc % 2] = ("te", p.cnt["te"])
                p.op("ve", "tensor_copy",
                     ffn_out.rearrange("p (s g c) -> p s g c", s=4, c=128)[:, :, dc, :],
                     pt3[:, 0:4, :], waits=(("te", p.cnt["te"]),), inc="ve")
                ptb_free[ptn] = ("ve", p.cnt["ve"])
            v_ffn_out = p.cnt["ve"]
            maybe_stop("I", ffn_out[:, :])

            p.op("sc", "preload_act_table", AF.Sqrt)
            nxt_xv = [None] * 6
            nrow0 = (t + 1) * TT
            for g in range(4):
                src = ffn_out[:, g * D:(g + 1) * D]
                col = 4 + (g % 2)
                v_rinv = ln_stats(col, src, dumpJ[:, :], (("ve", v_ffn_out),))
                p.op("ve", "tensor_scalar", snegmr[:, col:col + 1], smean[:, col:col + 1],
                     srinv[:, col:col + 1], -1.0, op0=ALU.mult, op1=ALU.mult,
                     waits=(("ve", v_rinv),), inc="ve")
                rg = r[:, g * D:(g + 1) * D]
                p.op("ve", "tensor_scalar", rg, rg, snegmr[:, col:col + 1], None, op0=ALU.add,
                     waits=(("ve", p.cnt["ve"]),), inc="ve")
                p.op("ve", "scalar_tensor_tensor", rg, src, srinv[:, col:col + 1], rg,
                     op0=ALU.mult, op1=ALU.add, waits=(("ve", p.cnt["ve"]),), inc="ve")
                p.op("sc", "dma_start", out_d[t * TT + g * 128: t * TT + (g + 1) * 128, :],
                     rg, waits=(("ve", p.cnt["ve"]),), inc=O_SEMS[g])
                if t + 1 < NT:
                    if g == 0:
                        nxt_xv[0] = ("p0", p.op("sy", "dma_start", halo[:, 0:D],
                                         xs[nrow0: nrow0 + 128, :],
                                         waits=(("te", p.cnt["te"]),),
                                         inc="p0"))
                        nxt_xv[5] = ("p5", p.op("sy", "dma_start", halo[:, D:2 * D],
                                         xs[nrow0 + 5 * 128: nrow0 + 6 * 128, :],
                                         inc="p5"))
                    ps_ = P_SEMS[g + 1]
                    nxt_xv[g + 1] = (ps_, p.op("sy", "dma_start", r[:, g * D:(g + 1) * D],
                                         xs[nrow0 + (g + 1) * 128: nrow0 + (g + 2) * 128, :],
                                         waits=((O_SEMS[g], p.cnt[O_SEMS[g]]),), inc=ps_))

        assert ws.acquired == len(ws.q), "weight stream not fully consumed"
        p.op("sy", "nop",
             waits=tuple((s, p.cnt[s]) for s in O_SEMS + P_SEMS)
             + (("x", p.cnt["x"]), ("y", p.cnt["y"])))


    try:
        _emit_tiles()
    except _Stop:
        pass

    with nc.Block() as block:
        @block.tensor
        def _(eng):
            p.replay("te", eng, sems)

        @block.vector
        def _(eng):
            p.replay("ve", eng, sems)

        @block.scalar
        def _(eng):
            p.replay("sc", eng, sems)

        @block.sync
        def _(eng):
            p.replay("sy", eng, sems)

        @block.gpsimd
        def _(eng):
            p.replay("gp", eng, sems)

    es.close()
    return nc


_NC_CACHE = None


def _get_nc():
    global _NC_CACHE
    if _NC_CACHE is None:
        _NC_CACHE = build_nc()
    return _NC_CACHE


def _prep(inputs):
    f32 = np.float32

    def a(name):
        return np.ascontiguousarray(np.asarray(inputs[name], f32))

    cw = a("L0_attn_conv_w")
    conv_s = cw.reshape(3, 32, 128, 24, 128).transpose(3, 0, 1, 2, 4)
    conv_s = np.ascontiguousarray(
        conv_s.reshape(24, 96, 128, 128).transpose(0, 2, 1, 3).reshape(24, 128, 96 * 128)
    )

    def qk_lay(w):
        wf = w.reshape(1024, 1024)
        return wf.reshape(8, 128, 8, 128).transpose(2, 1, 0, 3).reshape(8, 128, 1024)

    qk_s = np.ascontiguousarray(
        np.stack([qk_lay(a("L0_attn_wq")), qk_lay(a("L0_attn_wk"))]))
    wv_s = np.ascontiguousarray(
        a("L0_attn_wv").reshape(8, 128, 1024).transpose(1, 0, 2).reshape(128, 8192)
    )
    wo_s = np.ascontiguousarray(
        a("L0_attn_wo").reshape(8, 128, 8, 128).transpose(2, 1, 0, 3).reshape(8, 128, 1024)
    )
    pj_s = np.ascontiguousarray(
        a("L0_attn_proj_w").reshape(8, 128, 32, 128).transpose(2, 1, 0, 3).reshape(32, 128, 1024)
    )

    def gu_lay(w, b, row0):
        pad = np.zeros((86, 128, 128), f32)
        pad[:, row0:row0 + 16, :] = b.reshape(86, 128, 16).transpose(0, 2, 1)
        tiles = w.reshape(86, 128, 32, 128).transpose(0, 2, 3, 1)
        allt = np.concatenate([pad[:, None], tiles], axis=1)
        return allt.transpose(0, 2, 1, 3).reshape(86, 128, 33 * 128)

    gu_s = np.ascontiguousarray(np.stack([
        gu_lay(a("L0_ffn_gate_w"), a("L0_ffn_gate_b"), 0),
        gu_lay(a("L0_ffn_up_w"), a("L0_ffn_up_b"), 32),
    ], axis=1))

    dwpad = np.zeros((32, 128, 128), f32)
    dwpad[:, :16, :] = a("L0_ffn_down_b").reshape(32, 128, 16).transpose(0, 2, 1)
    dtiles = a("L0_ffn_down_w").reshape(32, 128, 86, 128).transpose(0, 2, 3, 1)
    dn_s = np.ascontiguousarray(
        np.concatenate([dwpad[:, None], dtiles], axis=1)
        .transpose(0, 2, 1, 3).reshape(32, 128, 87 * 128)
    )

    def aT_lay(w, nch):
        return w.reshape(16, nch, 128).transpose(2, 1, 0).reshape(128, nch * 16)

    lcon = np.ascontiguousarray(np.concatenate([
        aT_lay(a("L0_ffn_gate_a"), 32), aT_lay(a("L0_ffn_up_a"), 32),
        aT_lay(a("L0_ffn_down_a"), 86)], axis=1))

    biasb = np.zeros((128, 122), f32)
    biasb[:, 0:24] = a("L0_attn_conv_b").reshape(24, 128).T
    biasb[:, 24:32] = a("L0_attn_bq").reshape(8, 128).T
    biasb[:, 32:40] = a("L0_attn_bk").reshape(8, 128).T
    biasb[:, 40:48] = a("L0_attn_bv").reshape(8, 128).T
    biasb[:, 48:56] = a("L0_attn_bo").reshape(8, 128).T
    biasb[:, 56:88] = a("L0_attn_proj_b").reshape(32, 128).T
    biasb[0:64, 88] = 1.0
    biasb[64:128, 89] = 1.0
    bqf = a("L0_attn_bq").reshape(1024)
    bkf = a("L0_attn_bk").reshape(1024)
    for h in range(16):
        po = 64 * (h % 2)
        biasb[po:po + 64, 90 + h] = bqf[h * 64:(h + 1) * 64]
        biasb[po:po + 64, 106 + h] = bkf[h * 64:(h + 1) * 64]
    biasb = np.ascontiguousarray(biasb)

    shared = dict(conv_s=conv_s, qk_s=qk_s, wv_s=wv_s, wo_s=wo_s, pj_s=pj_s,
                  gu_s=gu_s, dn_s=dn_s, lcon=lcon, biasb=biasb)

    x = a("L0_attn_inputs").reshape(8192, 4096)
    in_maps = []
    for c in range(N_CORES):
        seq = c // 4
        base = c * 1024
        xs_c = np.zeros((1280, 4096), f32)
        lo = max(seq * 4096, base - 128)
        hi = min((seq + 1) * 4096, base + 1152)
        xs_c[lo - (base - 128): hi - (base - 128)] = x[lo:hi]
        m = dict(shared)
        m["xs"] = xs_c
        in_maps.append(m)
    return in_maps


def kernel(**inputs):
    in_maps = _prep(inputs)
    nc = _get_nc()
    res = run_bass_kernel_spmd(nc, in_maps, core_ids=list(range(N_CORES)))
    out = np.concatenate([res.results[c]["out"] for c in range(N_CORES)], axis=0)
    return out.reshape(2, 4096, 4096)


# version 08
# speedup vs baseline: 1.1453x; 1.0050x over previous
import numpy as np
import ml_dtypes
from contextlib import ExitStack

import concourse.bass as bass
import concourse.mybir as mybir
from concourse.bass_utils import run_bass_kernel_spmd

F32 = mybir.dt.float32
BF16 = mybir.dt.bfloat16
U8 = mybir.dt.uint8
AF = mybir.ActivationFunctionType
ALU = mybir.AluOpType
AX = mybir.AxisListType
BF = ml_dtypes.bfloat16

D = 4096
DC = 32
WQ = 1024
CC3 = 24
NH = 16
DH = 64
HID = 11008
MCH = 86
TT = 512
NT = 2
RK = 16
NEG = -1e9
EPS = 1e-5
N_CORES = 8

RING_SLOTS = 6
RING_COLS = 1152

ENGS = ("te", "ve", "sc", "sy", "gp")
W_SEMS = tuple(f"w{i}" for i in range(RING_SLOTS))
O_SEMS = tuple(f"o{i}" for i in range(4))
P_SEMS = tuple(f"p{i}" for i in range(6))
DMA_SEMS = W_SEMS + O_SEMS + P_SEMS + ("x", "y")


class Prog:

    def __init__(self):
        self.ops = {e: [] for e in ENGS}
        self.cnt = dict(x=0, y=0, te=0, ve=0, sc=0, gp=0)
        for s in O_SEMS + P_SEMS:
            self.cnt[s] = 0
        for s in W_SEMS:
            self.cnt[s] = 0
        self.lastwait = {e: {} for e in ENGS}

    def op(self, eng, meth, *args, waits=(), inc=None, **kwargs):
        wl = []
        for sem, val in waits:
            if not val:
                continue
            if sem in DMA_SEMS:
                val = self.cnt[sem]
            if self.lastwait[eng].get(sem, 0) >= val:
                continue
            self.lastwait[eng][sem] = val
            wl.append((sem, val))
        iv = None
        if inc:
            self.cnt[inc] += 16 if inc in DMA_SEMS else 1
            iv = self.cnt[inc]
        self.ops[eng].append((tuple(wl), meth, args, kwargs, inc))
        return iv

    def replay(self, ename, eng, sems):
        for wl, meth, args, kwargs, inc in self.ops[ename]:
            for sem, val in wl:
                eng.wait_ge(sems[sem], val)
            r = getattr(eng, meth)(*args, **kwargs)
            if inc:
                r.then_inc(sems[inc], 16 if inc in DMA_SEMS else 1)


class WStream:

    def __init__(self, p, buf_ap):
        self.p = p
        self.buf = buf_ap
        self.q = []
        self.freeval = [0] * RING_SLOTS
        self.issued = 0
        self.acquired = 0
        self.wv = {}

    def push(self, src, cols):
        assert cols <= RING_COLS
        self.q.append((src, cols))

    def _issue(self):
        i = self.issued
        src, cols = self.q[i]
        s = i % RING_SLOTS
        self.wv[i] = self.p.op(
            "gp", "dma_start",
            self.buf[:, s * RING_COLS: s * RING_COLS + cols], src,
            waits=(("te", self.freeval[s]),), inc=W_SEMS[s],
        )
        self.issued += 1

    def acquire(self):
        i = self.acquired
        while self.issued < min(i + RING_SLOTS, len(self.q)) :
            self._issue()
        self.acquired += 1
        s = i % RING_SLOTS
        cols = self.q[i][1]
        return (self.buf[:, s * RING_COLS: s * RING_COLS + cols],
                (W_SEMS[s], self.wv.pop(i)), s)

    def release(self, s):
        self.freeval[s] = self.p.cnt["te"]


def build_nc(stop=None):
    nc = bass.Bass("TRN2", target_bir_lowering=False)

    xs = nc.dram_tensor("xs", [1280, D], F32, kind="ExternalInput").ap()
    conv_s = nc.dram_tensor("conv_s", [CC3, 128, 96 * 128], F32, kind="ExternalInput").ap()
    qk_s = nc.dram_tensor("qk_s", [2, 8, 128, 1024], F32, kind="ExternalInput").ap()
    wv_s = nc.dram_tensor("wv_s", [128, 8192], F32, kind="ExternalInput").ap()
    wo_s = nc.dram_tensor("wo_s", [8, 128, 1024], F32, kind="ExternalInput").ap()
    pj_s = nc.dram_tensor("pj_s", [DC, 128, 1024], F32, kind="ExternalInput").ap()
    gu_s = nc.dram_tensor("gu_s", [MCH, 2, 128, 33 * 128], F32, kind="ExternalInput").ap()
    dn_s = nc.dram_tensor("dn_s", [DC, 128, 87 * 128], F32, kind="ExternalInput").ap()
    lcon = nc.dram_tensor("lcon", [128, 2400], F32, kind="ExternalInput").ap()
    biasb_in = nc.dram_tensor("biasb", [128, 122], F32, kind="ExternalInput").ap()
    out_d = nc.dram_tensor("out", [NT * TT, D], F32, kind="ExternalOutput").ap()

    es = ExitStack()

    def sb(name, shape, dtype):
        return es.enter_context(nc.sbuf_tensor(name, shape, dtype)).ap()

    def ps(name, shape, dtype):
        return es.enter_context(nc.psum_tensor(name, shape, dtype)).ap()

    r = sb("r", [128, 4 * D], F32)
    ring_t = sb("ring", [128, RING_SLOTS * RING_COLS], BF16)
    lora = sb("lora", [128, 2400], BF16)
    xabuf = sb("xabuf", [128, 512], BF16)
    biasb = sb("biasbuf", [128, 122], F32)
    id_f = sb("id_f", [128, 128], F32)
    id_b = sb("id_b", [128, 128], BF16)
    maskA = sb("maskA", [128, 128], F32)
    stage_dn = sb("stage_dn", [128, 1024], BF16)
    gsil = sb("gsil", [128, 1024], BF16)
    ssum = sb("ssum", [128, 8], F32)
    ssq = sb("ssq", [128, 8], F32)
    smean = sb("smean", [128, 8], F32)
    svar = sb("svar", [128, 8], F32)
    smsq = sb("smsq", [128, 8], F32)
    sstd = sb("sstd", [128, 8], F32)
    srinv = sb("srinv", [128, 8], F32)
    snegmr = sb("snegmr", [128, 8], F32)
    snegmax = sb("snegmax", [128, NH], F32)
    srowsum = sb("srowsum", [128, NH], F32)
    srinv16 = sb("srinv16", [128, NH], F32)

    POOL_BYTES = 120832
    pool_t = es.enter_context(nc.sbuf_tensor("pool", [128, POOL_BYTES], U8))
    pool_base = nc.lookup_mloc(pool_t).addr
    carves = []

    def carve(name, off, shape, dtype, phases):
        nbytes = int(np.prod(shape[1:])) * mybir.dt.size(dtype)
        assert off + nbytes <= POOL_BYTES, f"{name} overflows pool"
        for (n2, o2, b2, ph2) in carves:
            if off < o2 + b2 and o2 < off + nbytes and (phases & ph2):
                raise AssertionError(f"pool overlap {name} vs {n2}")
        carves.append((name, off, nbytes, phases))
        h = nc.alloc_sbuf_tensor_at(name, shape, dtype, offset=pool_base + off)
        return h.ap()

    PH = lambda *ks: frozenset(ks)
    xln = carve("xln", 0, [128, DC * 514], BF16, PH("A", "B"))
    stage_tm = carve("stage_tm", 32896, [128, 2 * D], BF16, PH("A", "G"))
    halo = carve("halo", 49280, [128, 2 * D], F32, PH("A", "J"))
    dumpA = carve("dumpA", 82048, [128, D], BF16, PH("A"))
    qkv = carve("qkv", 32896, [128, CC3 * 512], BF16, PH("B", "C"))
    wv_hold = carve("wv_hold", 57472, [128, 8192], BF16, PH("C"))
    qh = carve("qh", 73856, [128, 16 * 512], BF16, PH("C", "D"))
    kh = carve("kh", 90240, [128, 16 * 512], BF16, PH("C", "D"))
    vh = carve("vh", 106624, [128, 4 * 1024], BF16, PH("C", "D"))
    sm = carve("sm", 0, [128, NH * 128], F32, PH("D"))
    p_bf = carve("p_bf", 8192, [128, 2 * NH * 128], BF16, PH("D"))
    spT = carve("spT", 16384, [128, 2 * NH * 128], BF16, PH("D"))
    o_fm = carve("o_fm", 24576, [128, 8 * 512], BF16, PH("D", "E"))
    amix = carve("amix", 102400, [128, 8 * 512], BF16, PH("E"))
    stage_pf = carve("stage_pf", 32768, [128, 1024], F32, PH("E"))
    attn_out = carve("attn_out", 36864, [128, 4 * D], F32, PH("E", "F"))
    dumpF = carve("dumpF", 0, [128, D], BF16, PH("F"))
    xln2 = carve("xln2", 0, [128, DC * 512], BF16, PH("G", "H"))
    dumpG = carve("dumpG", 49280, [128, D], BF16, PH("G"))
    hidden = carve("hidden", 32768, [128, MCH * 512], BF16, PH("H", "I"))
    ffn_out = carve("ffn_out", 0, [128, 4 * D], BF16, PH("I", "J"))
    dumpJ = carve("dumpJ", 32768, [128, D], BF16, PH("J"))

    xln3 = xln.rearrange("p (c t) -> p c t", t=514)
    xln23 = xln2.rearrange("p (c t) -> p c t", t=512)

    pA = ps("pA", [128, 512], F32)
    pB = ps("pB", [128, 512], F32)
    pC = ps("pC", [128, 512], F32)
    pD = ps("pD", [128, 512], F32)
    pO1 = ps("pO1", [128, 512], F32)
    pO2 = ps("pO2", [128, 512], F32)
    pT1 = ps("pT1", [128, 1024], BF16)
    pT2 = ps("pT2", [128, 1024], BF16)
    scoreb = [pA, pB, pC, pD]

    sems = {}
    for s in W_SEMS + O_SEMS + P_SEMS + ("x", "y", "te", "ve", "sc", "gp"):
        sems[s] = es.enter_context(nc.semaphore("sem_" + s))

    p = Prog()
    ws = WStream(p, ring_t)
    bankfree = {}
    stfree = {0: ("te", 0), 1: ("te", 0)}
    gsil_free = {0: ("ve", 0), 1: ("ve", 0)}
    v_pT_par = {0: 0, 1: 0}
    v_o_par = {0: 0, 1: 0}
    v_spt_par_init = {}
    v_exp_prev = 0
    v_pT_prev = 0

    def bwait(*banks):
        return tuple(bankfree[b] for b in banks if b in bankfree)

    p.op("gp", "memset", xabuf[:, :], 0.0, inc="gp")
    p.op("gp", "memset", maskA[:, :], 0.0, inc="gp")
    p.op("gp", "affine_select", out=maskA[:, :], in_=maskA[:, :],
         compare_op=ALU.is_ge, fill=NEG, base=0, pattern=[[-1, 128]],
         channel_multiplier=1, waits=(("gp", p.cnt["gp"]),), inc="gp")
    p.op("gp", "memset", id_f[:, :], 0.0, inc="gp")
    p.op("gp", "affine_select", out=id_f[:, :], in_=id_f[:, :],
         compare_op=ALU.not_equal, fill=1.0, base=0, pattern=[[-1, 128]],
         channel_multiplier=1, waits=(("gp", p.cnt["gp"]),), inc="gp")
    v_gp = p.cnt["gp"]
    p.op("ve", "tensor_copy", id_b[:, :], id_f[:, :], waits=(("gp", v_gp),), inc="ve")
    v_idb = p.cnt["ve"]
    p.op("gp", "dma_start", lora[:, :], lcon, inc="y")
    v_lcon = p.cnt["y"]
    p.op("sc", "dma_start", biasb[:, :], biasb_in, inc="x")
    v_bias = p.cnt["x"]
    p.op("sc", "preload_act_table", AF.Sqrt)

    def ln_stats(col, src, dump, waits):
        p.op("ve", "reduce_sum", ssum[:, col:col + 1], src, axis=AX.X,
             waits=waits, inc="ve")
        v_sum = p.cnt["ve"]
        p.op("sc", "activation", dump, src, AF.Square,
             accum_out=ssq[:, col:col + 1], waits=waits, inc="sc")
        v_sq = p.cnt["sc"]
        p.op("ve", "tensor_scalar", smean[:, col:col + 1], ssum[:, col:col + 1],
             1.0 / D, None, op0=ALU.mult, waits=(("ve", v_sum),), inc="ve")
        v_mean = p.cnt["ve"]
        p.op("ve", "tensor_scalar", svar[:, col:col + 1], ssq[:, col:col + 1],
             1.0 / D, EPS, op0=ALU.mult, op1=ALU.add, waits=(("sc", v_sq),))
        p.op("ve", "tensor_tensor", smsq[:, col:col + 1], smean[:, col:col + 1],
             smean[:, col:col + 1], op=ALU.mult, waits=(("ve", v_mean),), inc="ve")
        v_msq = p.cnt["ve"]
        p.op("ve", "tensor_tensor", svar[:, col:col + 1], svar[:, col:col + 1],
             smsq[:, col:col + 1], op=ALU.subtract, waits=(("ve", v_msq),), inc="ve")
        p.op("sc", "activation", sstd[:, col:col + 1], svar[:, col:col + 1],
             AF.Sqrt, waits=(("ve", p.cnt["ve"]),), inc="sc")
        p.op("ve", "reciprocal", srinv[:, col:col + 1], sstd[:, col:col + 1],
             waits=(("sc", p.cnt["sc"]),), inc="ve")
        return p.cnt["ve"]

    ptb_free = {"pT1": ("ve", 0), "pT2": ("ve", 0)}

    def tm_to_fm(src_bf, dst3, dst_col_off, halo_col=None, waits=()):
        for grp in range(4):
            pt, ptn = (pT1, "pT1") if grp % 2 == 0 else (pT2, "pT2")
            pt3 = pt.rearrange("p (c t) -> p c t", t=128)
            for j in range(8):
                dc = grp * 8 + j
                w = waits if (grp == 0 and j == 0) else ()
                p.op("te", "transpose", pt3[:, j, :], src_bf[:, dc * 128:(dc + 1) * 128],
                     id_b[:, :],
                     waits=w + (ptb_free[ptn], ("ve", v_idb)),
                     inc="te" if j == 7 else None)
            v_t = p.cnt["te"]
            dc0 = grp * 8
            if halo_col is None:
                p.op("ve", "tensor_copy",
                     dst3[:, dc0:dc0 + 8, dst_col_off:dst_col_off + 128],
                     pt3[:, 0:8, :], waits=(("te", v_t),), inc="ve")
            else:
                p.op("ve", "tensor_copy",
                     dst3[:, dc0:dc0 + 8, dst_col_off:dst_col_off + 1],
                     pt3[:, 0:8, halo_col:halo_col + 1],
                     waits=(("te", v_t),), inc="ve")
            ptb_free[ptn] = ("ve", p.cnt["ve"])
        return p.cnt["ve"]

    class _Stop(Exception):
        pass

    stopped = []

    def dbg_dump(*aps):
        for i, apx in enumerate(aps):
            d = nc.dram_tensor(f"dbg{i}", list(apx.shape), apx.dtype,
                               kind="ExternalOutput").ap()
            p.op("sc", "dma_start", d, apx,
                 waits=(("ve", p.cnt["ve"]), ("sc", p.cnt["sc"]),
                        ("te", p.cnt["te"]), ("w", p.cnt["w"]), ("gp", p.cnt["gp"])),
                 inc="x")
        p.op("sy", "nop", waits=(("x", p.cnt["x"]),))
        stopped.append(True)
        raise _Stop()

    def maybe_stop(tag, *aps):
        if stop == tag:
            dbg_dump(*aps)

    def _emit_tiles():
        nonlocal v_exp_prev, v_pT_prev
        for t in range(NT):
            row0 = t * TT

            for cc in range(CC3):
                for b in range(12):
                    ws.push(conv_s[cc, :, b * 1024:(b + 1) * 1024], 1024)
            for pr in range(2):
                for hc in range(8):
                    ws.push(qk_s[pr, hc, :, :], 1024)
            for cc in range(8):
                ws.push(wo_s[cc, :, :], 1024)
            for dc in range(DC):
                ws.push(pj_s[dc, :, :], 1024)
            for mc in range(MCH):
                for gu in range(2):
                    ws.push(gu_s[mc, gu, :, 0:1152], 1152)
                    for b in range(3):
                        ws.push(gu_s[mc, gu, :, 1152 + b * 1024: 1152 + (b + 1) * 1024], 1024)
            for dc in range(DC):
                for b in range(10):
                    ws.push(dn_s[dc, :, b * 1024:(b + 1) * 1024], 1024)
                ws.push(dn_s[dc, :, 10240:11136], 896)

            if t == 0:
                xv = []
                for g in range(6):
                    if g == 0:
                        dst = halo[:, 0:D]
                    elif g == 5:
                        dst = halo[:, D:2 * D]
                    else:
                        dst = r[:, (g - 1) * D: g * D]
                    xv.append(("x", p.op("sc", "dma_start", dst,
                                   xs[row0 + g * 128: row0 + (g + 1) * 128, :],
                                   waits=(("x", p.cnt["x"]), ("gp", v_gp)), inc="x")))
            else:
                xv = nxt_xv
            v_xln = 0
            for g in [0, 5, 1, 2, 3, 4]:
                if g == 0:
                    src = halo[:, 0:D]
                elif g == 5:
                    src = halo[:, D:2 * D]
                else:
                    src = r[:, (g - 1) * D: g * D]
                wv = (xv[g],)
                v_rinv = ln_stats(g, src, dumpA[:, :], wv)
                st = stage_tm[:, (g % 2) * D: (g % 2 + 1) * D]
                p.op("ve", "tensor_scalar", st, src, smean[:, g:g + 1],
                     srinv[:, g:g + 1], op0=ALU.subtract, op1=ALU.mult,
                     waits=(stfree[g % 2], ("ve", v_rinv)), inc="ve")
                v_app = p.cnt["ve"]
                if g == 0:
                    v_xln = tm_to_fm(st, xln3, 0, halo_col=127, waits=(("ve", v_app),))
                elif g == 5:
                    v_xln = tm_to_fm(st, xln3, 513, halo_col=0, waits=(("ve", v_app),))
                else:
                    v_xln = tm_to_fm(st, xln3, 1 + (g - 1) * 128, waits=(("ve", v_app),))
                stfree[g % 2] = ("te", p.cnt["te"])
            v_halo_done = v_xln
            maybe_stop("A", xln[:, :], r[:, :])

            p.op("gp", "dma_start", wv_hold[:, 0:4096], wv_s[:, 0:4096],
                 waits=(("ve", v_halo_done),), inc="y")
            p.op("gp", "dma_start", wv_hold[:, 4096:8192], wv_s[:, 4096:8192], inc="y")
            v_wv = p.cnt["y"]

            for cc in range(CC3):
                pb, pbn = (pA, "pA") if cc % 2 == 0 else (pB, "pB")
                first_w = bwait(pbn) + (("ve", v_xln),)
                for b in range(12):
                    slot, wv, si = ws.acquire()
                    for ti in range(8):
                        tt_i = b * 8 + ti
                        k, dc = tt_i // 32, tt_i % 32
                        p.op("te", "matmul", pb[:, :], slot[:, ti * 128:(ti + 1) * 128],
                             xln3[:, dc, k:k + TT],
                             start=(tt_i == 0), stop=(tt_i == 95),
                             waits=(wv,) + (first_w if tt_i == 0 else ()),
                             inc="te" if ti == 7 else None)
                    ws.release(si)
                v_mm = p.cnt["te"]
                p.op("sc", "activation", qkv[:, cc * 512:(cc + 1) * 512], pb[:, :],
                     AF.Identity, bias=biasb[:, cc:cc + 1],
                     waits=(("te", v_mm), ("x", v_bias)), inc="sc")
                bankfree[pbn] = ("sc", p.cnt["sc"])
            v_qkv = p.cnt["sc"]
            maybe_stop("B", qkv[:, :], xln[:, :])

            for pr in range(2):
                dstb = qh if pr == 0 else kh
                bias_off = 24 + 8 * pr
                for hc in range(8):
                    pb, pbn = (pC, "pC") if hc % 2 == 0 else (pD, "pD")
                    slot, wv, si = ws.acquire()
                    for kc in range(8):
                        p.op("te", "matmul", pb[:, :], slot[:, kc * 128:(kc + 1) * 128],
                             qkv[:, (pr * 8 + kc) * 512:(pr * 8 + kc + 1) * 512],
                             start=(kc == 0), stop=(kc == 7),
                             waits=(wv, ("sc", v_qkv)) + (bwait(pbn) if kc == 0 else ()),
                             inc="te" if kc == 7 else None)
                    ws.release(si)
                    v_mm = p.cnt["te"]
                    for half in range(2):
                        h = 2 * hc + half
                        p.op("ve", "tensor_scalar",
                             dstb[:, h * 512:(h + 1) * 512], pb[:, :],
                             biasb[:, 88 + half:89 + half],
                             biasb[:, 90 + 16 * pr + h:91 + 16 * pr + h],
                             op0=ALU.mult, op1=ALU.add,
                             waits=(("te", v_mm), ("x", 1)),
                             inc="ve" if half == 1 else None)
                    bankfree[pbn] = ("ve", p.cnt["ve"])
            v_qkh = p.cnt["ve"]
            for tg in range(4):
                for hf in range(2):
                    i = tg * 2 + hf
                    pb, pbn = (pA, "pA") if i % 2 == 0 else (pB, "pB")
                    for kc in range(8):
                        p.op("te", "matmul", pb[:, :],
                             qkv[:, (16 + kc) * 512 + tg * 128: (16 + kc) * 512 + (tg + 1) * 128],
                             wv_hold[:, kc * 1024 + hf * 512: kc * 1024 + (hf + 1) * 512],
                             start=(kc == 0), stop=(kc == 7),
                             waits=(("y", v_wv), ("sc", v_qkv)) + (bwait(pbn) if kc == 0 else ()),
                             inc="te" if kc == 7 else None)
                    p.op("ve", "tensor_copy",
                         vh[:, tg * 1024 + hf * 512: tg * 1024 + (hf + 1) * 512],
                         pb[:, :], waits=(("te", p.cnt["te"]),), inc="ve")
                    bankfree[pbn] = ("ve", p.cnt["ve"])
            v_vh = p.cnt["ve"]
            maybe_stop("C", qh[:, :], kh[:, :], vh[:, :])

            p.op("sc", "preload_act_table", AF.Exp)
            sm3 = sm.rearrange("p (h c) -> p h c", c=128)
            v_norm = {}
            v_spt_par = dict(v_spt_par_init)
            for it in range(5):
                if it < 4:
                    b = it
                    pbuf = p_bf[:, (b % 2) * 2048:(b % 2 + 1) * 2048]
                    bank_end = []
                    for h in range(NH):
                        bank = scoreb[h // 4]
                        w = (("ve", v_qkh),)
                        if h % 4 == 0:
                            w = w + bwait("pA pB pC pD".split()[h // 4])
                        p.op("te", "matmul", bank[:, (h % 4) * 128:(h % 4 + 1) * 128],
                             qh[:, h * 512 + b * 128: h * 512 + (b + 1) * 128],
                             kh[:, h * 512 + b * 128: h * 512 + (b + 1) * 128],
                             start=True, stop=True, waits=w,
                             inc="te" if h % 4 == 3 else None)
                        if h % 4 == 3:
                            bank_end.append(p.cnt["te"])
                        if h in (7, 15):
                            for bk in ((0, 1) if h == 7 else (2, 3)):
                                p.op("ve", "scalar_tensor_tensor",
                                     sm3[:, bk * 4:(bk + 1) * 4, :],
                                     scoreb[bk].rearrange("p (s c) -> p s c", c=128),
                                     0.125,
                                     maskA.unsqueeze(1).broadcast_to([128, 4, 128]),
                                     op0=ALU.mult, op1=ALU.add,
                                     waits=(("te", bank_end[bk]), ("gp", v_gp),
                                            ("sc", v_exp_prev)),
                                     inc="ve")
                                bankfree["pA pB pC pD".split()[bk]] = ("ve", p.cnt["ve"])
                    if b == 0:
                        maybe_stop("D1", sm[:, :])
                    p.op("ve", "reduce_max", snegmax[:, 0:1], sm[:, :], axis=AX.X,
                         negate=True, waits=(("ve", p.cnt["ve"]),), inc="ve")
                    v_max = p.cnt["ve"]
                    p.op("sc", "activation", pbuf[:, :], sm[:, :], AF.Exp,
                         bias=snegmax[:, 0:1],
                         waits=(("ve", v_max), ("te", v_pT_par[b % 2])), inc="sc")
                    v_exp_prev = p.cnt["sc"]
                    p.op("ve", "reduce_sum", srowsum[:, :],
                         pbuf[:, :].rearrange("p (h c) -> p h c", c=128), axis=AX.X,
                         waits=(("sc", v_exp_prev),), inc="ve")
                    p.op("ve", "reciprocal", srinv16[:, :], srowsum[:, :],
                         waits=(("ve", p.cnt["ve"]),), inc="ve")
                    v_recip = p.cnt["ve"]
                    for h in range(NH):
                        p.op("ve", "tensor_scalar", pbuf[:, h * 128:(h + 1) * 128],
                             pbuf[:, h * 128:(h + 1) * 128], srinv16[:, h:h + 1], None,
                             op0=ALU.mult, waits=(("ve", v_recip),),
                             inc="ve" if h == NH - 1 else None)
                    v_norm[b] = p.cnt["ve"]
                if it >= 1:
                    b = it - 1
                    pbuf = p_bf[:, (b % 2) * 2048:(b % 2 + 1) * 2048]
                    spt = spT[:, (b % 2) * 2048:(b % 2 + 1) * 2048]
                    for h in range(NH):
                        pt, ptn = (pT1, "pT1") if h < 8 else (pT2, "pT2")
                        p.op("te", "transpose", pt[:, (h % 8) * 128:(h % 8 + 1) * 128],
                             pbuf[:, h * 128:(h + 1) * 128], id_b[:, :],
                             waits=(("ve", v_norm[b]),) + ((ptb_free[ptn],) if h % 8 == 0 else ()),
                             inc="te" if h % 8 == 7 else None)
                        if h % 8 == 7:
                            v_tp = p.cnt["te"]
                            p.op("ve", "tensor_copy", spt[:, (h - 7) * 128:(h + 1) * 128],
                                 pt[:, :],
                                 waits=(("te", v_tp), ("te", v_o_par[b % 2])), inc="ve")
                            ptb_free[ptn] = ("ve", p.cnt["ve"])
                    v_spt_b = p.cnt["ve"]
                    v_pT_par[b % 2] = p.cnt["te"]
                    for h in range(NH):
                        pb, pbn = (pO1, "pO1") if h < 8 else (pO2, "pO2")
                        po = 64 * (h % 2)
                        col = ((h // 2) % 4) * 128
                        w = (("ve", v_spt_b), ("ve", v_vh))
                        if h % 8 == 0:
                            w = w + bwait(pbn)
                        p.op("te", "matmul", pb[po:po + 64, col:col + 128],
                             vh[:, b * 1024 + h * 64: b * 1024 + (h + 1) * 64],
                             spt[:, h * 128:(h + 1) * 128],
                             start=True, stop=True, tile_position=(0, po), waits=w,
                             inc="te" if h % 8 == 7 else None)
                        if h % 8 == 7:
                            v_o = p.cnt["te"]
                            for j in range(4):
                                hc = (h // 8) * 4 + j
                                p.op("sc", "activation",
                                     o_fm[:, hc * 512 + b * 128: hc * 512 + (b + 1) * 128],
                                     pb[:, j * 128:(j + 1) * 128], AF.Identity,
                                     bias=biasb[:, 40 + hc:40 + hc + 1],
                                     waits=(("te", v_o),), inc="sc" if j == 3 else None)
                            bankfree[pbn] = ("sc", p.cnt["sc"])
                    v_o_par[b % 2] = p.cnt["te"]
            v_o_all = p.cnt["sc"]
            maybe_stop("D", o_fm[:, :])

            for cc in range(8):
                pb, pbn = (pC, "pC") if cc % 2 == 0 else (pD, "pD")
                slot, wv, si = ws.acquire()
                for kc in range(8):
                    p.op("te", "matmul", pb[:, :], slot[:, kc * 128:(kc + 1) * 128],
                         o_fm[:, kc * 512:(kc + 1) * 512],
                         start=(kc == 0), stop=(kc == 7),
                         waits=(wv, ("sc", v_o_all)) + (bwait(pbn) if kc == 0 else ()),
                         inc="te" if kc == 7 else None)
                ws.release(si)
                p.op("sc", "activation", amix[:, cc * 512:(cc + 1) * 512], pb[:, :],
                     AF.Identity, bias=biasb[:, 48 + cc:48 + cc + 1],
                     waits=(("te", p.cnt["te"]),), inc="sc")
                bankfree[pbn] = ("sc", p.cnt["sc"])
            v_amix = p.cnt["sc"]
            stagef_free = [("sc", 0), ("sc", 0)]
            for dc in range(DC):
                pb, pbn = (pA, "pA") if dc % 2 == 0 else (pB, "pB")
                slot, wv, si = ws.acquire()
                for kc in range(8):
                    p.op("te", "matmul", pb[:, :], slot[:, kc * 128:(kc + 1) * 128],
                         amix[:, kc * 512:(kc + 1) * 512],
                         start=(kc == 0), stop=(kc == 7),
                         waits=(wv, ("sc", v_amix)) + (bwait(pbn) if kc == 0 else ()),
                         inc="te" if kc == 7 else None)
                ws.release(si)
                v_mm = p.cnt["te"]
                sf = stage_pf[:, (dc % 2) * 512:(dc % 2 + 1) * 512]
                p.op("sc", "activation", sf, pb[:, :], AF.Identity,
                     bias=biasb[:, 56 + dc:56 + dc + 1],
                     waits=(("te", v_mm), stagef_free[dc % 2]), inc="sc")
                v_st = p.cnt["sc"]
                bankfree[pbn] = ("sc", v_st)
                pt, ptn = (pO1, "pO1") if dc % 2 == 0 else (pO2, "pO2")
                pt3 = pt.rearrange("p (s c) -> p s c", c=128)
                for tg in range(4):
                    p.op("te", "transpose", pt3[:, tg, :], sf[:, tg * 128:(tg + 1) * 128],
                         id_f[:, :],
                         waits=(("sc", v_st),) + (bwait(ptn) if tg == 0 else ()),
                         inc="te" if tg == 3 else None)
                stagef_free[dc % 2] = ("te", p.cnt["te"])
                p.op("ve", "tensor_copy",
                     attn_out.rearrange("p (s g c) -> p s g c", s=4, c=128)[:, :, dc, :],
                     pt3[:, :, :], waits=(("te", p.cnt["te"]),), inc="ve")
                bankfree[ptn] = ("ve", p.cnt["ve"])
            v_attn_out = p.cnt["ve"]
            maybe_stop("E", amix[:, :], attn_out[:, :])

            p.op("sc", "preload_act_table", AF.Sqrt)
            v_r2 = []
            for g in range(4):
                src = attn_out[:, g * D:(g + 1) * D]
                v_rinv = ln_stats(g, src, dumpF[:, :], (("ve", v_attn_out),))
                p.op("ve", "tensor_scalar", snegmr[:, g:g + 1], smean[:, g:g + 1],
                     srinv[:, g:g + 1], -1.0, op0=ALU.mult, op1=ALU.mult,
                     waits=(("ve", v_rinv),), inc="ve")
                rg = r[:, g * D:(g + 1) * D]
                p.op("ve", "tensor_scalar", rg, rg, snegmr[:, g:g + 1], None, op0=ALU.add,
                     waits=(("ve", p.cnt["ve"]),), inc="ve")
                p.op("ve", "scalar_tensor_tensor", rg, src, srinv[:, g:g + 1], rg,
                     op0=ALU.mult, op1=ALU.add, waits=(("ve", p.cnt["ve"]),), inc="ve")
                v_r2.append(p.cnt["ve"])

            maybe_stop("F", r[:, :])

            for g in range(4):
                rg = r[:, g * D:(g + 1) * D]
                v_rinv = ln_stats(g, rg, dumpG[:, :], (("ve", v_r2[g]),))
                st = stage_tm[:, (g % 2) * D: (g % 2 + 1) * D]
                p.op("ve", "tensor_scalar", st, rg, smean[:, g:g + 1],
                     srinv[:, g:g + 1], op0=ALU.subtract, op1=ALU.mult,
                     waits=(stfree[g % 2], ("ve", v_rinv)), inc="ve")
                v_app = p.cnt["ve"]
                v_xln2 = tm_to_fm(st, xln23, g * 128, waits=(("ve", v_app),))
                stfree[g % 2] = ("te", p.cnt["te"])

            maybe_stop("G", xln2[:, :])

            p.op("sc", "preload_act_table", AF.Silu)
            for i, (off, po) in enumerate([(0, 0), (512, 32)]):
                pb, pbn = (pO1, "pO1") if i == 0 else (pO2, "pO2")
                for kc in range(DC):
                    p.op("te", "matmul", pb[po:po + RK, :],
                         lora[:, off + kc * RK: off + (kc + 1) * RK],
                         xln23[:, kc, :],
                         start=(kc == 0), stop=(kc == DC - 1),
                         waits=(("ve", v_xln2), ("y", v_lcon)) + (bwait(pbn) if kc == 0 else ()),
                         tile_position=(0, po) if po else None,
                         inc="te" if kc == DC - 1 else None)
                p.op("ve", "tensor_copy", xabuf[po:po + RK, :], pb[po:po + RK, :],
                     waits=(("te", p.cnt["te"]),), inc="ve")
                bankfree[pbn] = ("ve", p.cnt["ve"])
            v_xa = p.cnt["ve"]
            for mc in range(MCH):
                banks = [(pA, "pA"), (pC, "pC")] if mc % 2 == 0 else [(pB, "pB"), (pD, "pD")]
                for gu in range(2):
                    pb, pbn = banks[gu]
                    nt_done = 0
                    for blk in range(4):
                        slot, wv, si = ws.acquire()
                        ntile = slot.shape[-1] // 128
                        for ti in range(ntile):
                            if nt_done == 0:
                                rhs = xabuf[:, :]
                                w = (wv, ("ve", v_xa)) + bwait(pbn)
                            else:
                                rhs = xln23[:, nt_done - 1, :]
                                w = (wv,) if ti == 0 else ()
                            p.op("te", "matmul", pb[:, :],
                                 slot[:, ti * 128:(ti + 1) * 128], rhs,
                                 start=(nt_done == 0), stop=(nt_done == 32), waits=w,
                                 inc="te" if ti == ntile - 1 else None)
                            nt_done += 1
                        ws.release(si)
                v_gu = p.cnt["te"]
                pg, pgn = banks[0]
                pu, pun = banks[1]
                gs = gsil[:, (mc % 2) * 512:(mc % 2 + 1) * 512]
                p.op("sc", "activation", gs, pg[:, :], AF.Silu,
                     waits=(("te", v_gu),) + (gsil_free[mc % 2],), inc="sc")
                p.op("ve", "tensor_tensor", hidden[:, mc * 512:(mc + 1) * 512],
                     gs, pu[:, :], op=ALU.mult,
                     waits=(("sc", p.cnt["sc"]),), inc="ve")
                gsil_free[mc % 2] = ("ve", p.cnt["ve"])
                for n in (pgn, pun):
                    bankfree[n] = ("ve", p.cnt["ve"])
            v_hid = p.cnt["ve"]
            maybe_stop("H", hidden[:, :], xabuf[:, :])

            for kc in range(MCH):
                p.op("te", "matmul", pO1[0:RK, :],
                     lora[:, 1024 + kc * RK: 1024 + (kc + 1) * RK],
                     hidden[:, kc * 512:(kc + 1) * 512],
                     start=(kc == 0), stop=(kc == MCH - 1),
                     waits=(("ve", v_hid),) + (bwait("pO1") if kc == 0 else ()),
                     inc="te" if kc == MCH - 1 else None)
            p.op("ve", "tensor_copy", xabuf[0:RK, :], pO1[0:RK, :],
                 waits=(("te", p.cnt["te"]),), inc="ve")
            bankfree["pO1"] = ("ve", p.cnt["ve"])
            v_xd = p.cnt["ve"]
            stagedn_free = [("te", 0), ("te", 0)]
            for dc in range(DC):
                pb, pbn = (pA, "pA") if dc % 2 == 0 else (pB, "pB")
                nt_done = 0
                for b in range(11):
                    slot, wv, si = ws.acquire()
                    ntile = slot.shape[-1] // 128
                    for ti in range(ntile):
                        if nt_done == 0:
                            rhs = xabuf[:, :]
                            w = (wv, ("ve", v_xd)) + bwait(pbn)
                        else:
                            rhs = hidden[:, (nt_done - 1) * 512: nt_done * 512]
                            w = (wv,) if ti == 0 else ()
                        p.op("te", "matmul", pb[:, :], slot[:, ti * 128:(ti + 1) * 128],
                             rhs, start=(nt_done == 0), stop=(nt_done == 86), waits=w,
                             inc="te" if ti == ntile - 1 else None)
                        nt_done += 1
                    ws.release(si)
                v_mm = p.cnt["te"]
                sf = stage_dn[:, (dc % 2) * 512:(dc % 2 + 1) * 512]
                p.op("sc", "copy", sf, pb[:, :],
                     waits=(("te", v_mm), stagedn_free[dc % 2]), inc="sc")
                v_st = p.cnt["sc"]
                bankfree[pbn] = ("sc", v_st)
                pt, ptn = (pT1, "pT1") if dc % 2 == 0 else (pT2, "pT2")
                pt3 = pt.rearrange("p (s c) -> p s c", c=128)
                for tg in range(4):
                    p.op("te", "transpose", pt3[:, tg, :], sf[:, tg * 128:(tg + 1) * 128],
                         id_b[:, :],
                         waits=(("sc", v_st),) + ((ptb_free[ptn],) if tg == 0 else ()),
                         inc="te" if tg == 3 else None)
                stagedn_free[dc % 2] = ("te", p.cnt["te"])
                p.op("ve", "tensor_copy",
                     ffn_out.rearrange("p (s g c) -> p s g c", s=4, c=128)[:, :, dc, :],
                     pt3[:, 0:4, :], waits=(("te", p.cnt["te"]),), inc="ve")
                ptb_free[ptn] = ("ve", p.cnt["ve"])
            v_ffn_out = p.cnt["ve"]
            maybe_stop("I", ffn_out[:, :])

            p.op("sc", "preload_act_table", AF.Sqrt)
            nxt_xv = [None] * 6
            nrow0 = (t + 1) * TT
            for g in range(4):
                src = ffn_out[:, g * D:(g + 1) * D]
                col = 4 + (g % 2)
                v_rinv = ln_stats(col, src, dumpJ[:, :], (("ve", v_ffn_out),))
                p.op("ve", "tensor_scalar", snegmr[:, col:col + 1], smean[:, col:col + 1],
                     srinv[:, col:col + 1], -1.0, op0=ALU.mult, op1=ALU.mult,
                     waits=(("ve", v_rinv),), inc="ve")
                rg = r[:, g * D:(g + 1) * D]
                p.op("ve", "tensor_scalar", rg, rg, snegmr[:, col:col + 1], None, op0=ALU.add,
                     waits=(("ve", p.cnt["ve"]),), inc="ve")
                p.op("ve", "scalar_tensor_tensor", rg, src, srinv[:, col:col + 1], rg,
                     op0=ALU.mult, op1=ALU.add, waits=(("ve", p.cnt["ve"]),), inc="ve")
                p.op("sc", "dma_start", out_d[t * TT + g * 128: t * TT + (g + 1) * 128, :],
                     rg, waits=(("ve", p.cnt["ve"]),), inc=O_SEMS[g])
                if t + 1 < NT:
                    if g == 0:
                        nxt_xv[0] = ("p0", p.op("sy", "dma_start", halo[:, 0:D],
                                         xs[nrow0: nrow0 + 128, :],
                                         waits=(("te", p.cnt["te"]),),
                                         inc="p0"))
                        nxt_xv[5] = ("p5", p.op("sy", "dma_start", halo[:, D:2 * D],
                                         xs[nrow0 + 5 * 128: nrow0 + 6 * 128, :],
                                         inc="p5"))
                    ps_ = P_SEMS[g + 1]
                    nxt_xv[g + 1] = (ps_, p.op("sy", "dma_start", r[:, g * D:(g + 1) * D],
                                         xs[nrow0 + (g + 1) * 128: nrow0 + (g + 2) * 128, :],
                                         waits=((O_SEMS[g], p.cnt[O_SEMS[g]]),), inc=ps_))

        assert ws.acquired == len(ws.q), "weight stream not fully consumed"
        p.op("sy", "nop",
             waits=tuple((s, p.cnt[s]) for s in O_SEMS + P_SEMS)
             + (("x", p.cnt["x"]), ("y", p.cnt["y"])))


    try:
        _emit_tiles()
    except _Stop:
        pass

    with nc.Block() as block:
        @block.tensor
        def _(eng):
            p.replay("te", eng, sems)

        @block.vector
        def _(eng):
            p.replay("ve", eng, sems)

        @block.scalar
        def _(eng):
            p.replay("sc", eng, sems)

        @block.sync
        def _(eng):
            p.replay("sy", eng, sems)

        @block.gpsimd
        def _(eng):
            p.replay("gp", eng, sems)

    es.close()
    return nc


_NC_CACHE = None


def _get_nc():
    global _NC_CACHE
    if _NC_CACHE is None:
        _NC_CACHE = build_nc()
    return _NC_CACHE


def _prep(inputs):
    f32 = np.float32

    def a(name):
        return np.ascontiguousarray(np.asarray(inputs[name], f32))

    cw = a("L0_attn_conv_w")
    conv_s = cw.reshape(3, 32, 128, 24, 128).transpose(3, 0, 1, 2, 4)
    conv_s = np.ascontiguousarray(
        conv_s.reshape(24, 96, 128, 128).transpose(0, 2, 1, 3).reshape(24, 128, 96 * 128)
    )

    def qk_lay(w):
        wf = w.reshape(1024, 1024)
        return wf.reshape(8, 128, 8, 128).transpose(2, 1, 0, 3).reshape(8, 128, 1024)

    qk_s = np.ascontiguousarray(
        np.stack([qk_lay(a("L0_attn_wq")), qk_lay(a("L0_attn_wk"))]))
    wv_s = np.ascontiguousarray(
        a("L0_attn_wv").reshape(8, 128, 1024).transpose(1, 0, 2).reshape(128, 8192)
    )
    wo_s = np.ascontiguousarray(
        a("L0_attn_wo").reshape(8, 128, 8, 128).transpose(2, 1, 0, 3).reshape(8, 128, 1024)
    )
    pj_s = np.ascontiguousarray(
        a("L0_attn_proj_w").reshape(8, 128, 32, 128).transpose(2, 1, 0, 3).reshape(32, 128, 1024)
    )

    def gu_lay(w, b, row0):
        pad = np.zeros((86, 128, 128), f32)
        pad[:, row0:row0 + 16, :] = b.reshape(86, 128, 16).transpose(0, 2, 1)
        tiles = w.reshape(86, 128, 32, 128).transpose(0, 2, 3, 1)
        allt = np.concatenate([pad[:, None], tiles], axis=1)
        return allt.transpose(0, 2, 1, 3).reshape(86, 128, 33 * 128)

    gu_s = np.ascontiguousarray(np.stack([
        gu_lay(a("L0_ffn_gate_w"), a("L0_ffn_gate_b"), 0),
        gu_lay(a("L0_ffn_up_w"), a("L0_ffn_up_b"), 32),
    ], axis=1))

    dwpad = np.zeros((32, 128, 128), f32)
    dwpad[:, :16, :] = a("L0_ffn_down_b").reshape(32, 128, 16).transpose(0, 2, 1)
    dtiles = a("L0_ffn_down_w").reshape(32, 128, 86, 128).transpose(0, 2, 3, 1)
    dn_s = np.ascontiguousarray(
        np.concatenate([dwpad[:, None], dtiles], axis=1)
        .transpose(0, 2, 1, 3).reshape(32, 128, 87 * 128)
    )

    def aT_lay(w, nch):
        return w.reshape(16, nch, 128).transpose(2, 1, 0).reshape(128, nch * 16)

    lcon = np.ascontiguousarray(np.concatenate([
        aT_lay(a("L0_ffn_gate_a"), 32), aT_lay(a("L0_ffn_up_a"), 32),
        aT_lay(a("L0_ffn_down_a"), 86)], axis=1))

    biasb = np.zeros((128, 122), f32)
    biasb[:, 0:24] = a("L0_attn_conv_b").reshape(24, 128).T
    biasb[:, 24:32] = a("L0_attn_bq").reshape(8, 128).T
    biasb[:, 32:40] = a("L0_attn_bk").reshape(8, 128).T
    biasb[:, 40:48] = a("L0_attn_bv").reshape(8, 128).T
    biasb[:, 48:56] = a("L0_attn_bo").reshape(8, 128).T
    biasb[:, 56:88] = a("L0_attn_proj_b").reshape(32, 128).T
    biasb[0:64, 88] = 1.0
    biasb[64:128, 89] = 1.0
    bqf = a("L0_attn_bq").reshape(1024)
    bkf = a("L0_attn_bk").reshape(1024)
    for h in range(16):
        po = 64 * (h % 2)
        biasb[po:po + 64, 90 + h] = bqf[h * 64:(h + 1) * 64]
        biasb[po:po + 64, 106 + h] = bkf[h * 64:(h + 1) * 64]
    biasb = np.ascontiguousarray(biasb)

    shared = dict(conv_s=conv_s, qk_s=qk_s, wv_s=wv_s, wo_s=wo_s, pj_s=pj_s,
                  gu_s=gu_s, dn_s=dn_s, lcon=lcon, biasb=biasb)

    x = a("L0_attn_inputs").reshape(8192, 4096)
    in_maps = []
    for c in range(N_CORES):
        seq = c // 4
        base = c * 1024
        xs_c = np.zeros((1280, 4096), f32)
        lo = max(seq * 4096, base - 128)
        hi = min((seq + 1) * 4096, base + 1152)
        xs_c[lo - (base - 128): hi - (base - 128)] = x[lo:hi]
        m = dict(shared)
        m["xs"] = xs_c
        in_maps.append(m)
    return in_maps


def kernel(**inputs):
    in_maps = _prep(inputs)
    nc = _get_nc()
    res = run_bass_kernel_spmd(nc, in_maps, core_ids=list(range(N_CORES)))
    out = np.concatenate([res.results[c]["out"] for c in range(N_CORES)], axis=0)
    return out.reshape(2, 4096, 4096)
